# version 07
# baseline (speedup 1.0000x reference)
import os as _os
import numpy as np
import ml_dtypes
import concourse.bass as bass
import concourse.mybir as mybir
from concourse.bass_utils import run_bass_kernel_spmd
from contextlib import ExitStack

F32 = mybir.dt.float32
BF16 = mybir.dt.bfloat16
AF = mybir.ActivationFunctionType
ALU = mybir.AluOpType
AX = mybir.AxisListType

NCORES = 8
TL = 1024
D = 2048
NS = 8
FC = 16
H1 = 4 * D
LN8 = float(np.log(8.0))
EPS_LN = 1e-5
EPS_RMS = 1.1920929e-7
THR_C = 0.5 * (1.0 + 2.0 ** -9)

STOP_AFTER = None


class MK:
    def __init__(self, nc, block, es):
        self.nc, self.block, self.es = nc, block, es
        self.n, self.h = {}, {}
        for name in ("t", "v", "s", "c", "y", "g0", "g1", "gm", "gb", "yb", "c0", "c1", "w0"):
            self.h[name] = es.enter_context(nc.semaphore("sem_" + name))
            self.n[name] = 0

    def t(self, f): self.block.tensor(f)
    def v(self, f): self.block.vector(f)
    def s(self, f): self.block.scalar(f)
    def g(self, f): self.block.gpsimd(f)
    def y(self, f): self.block.sync(f)

    def inc(self, instr, sem, k=1):
        instr.then_inc(self.h[sem], k)
        self.n[sem] += k
        return self.n[sem]

    def wait(self, eng, sem, val):
        if val > 0:
            eng.wait_ge(self.h[sem], val)

    def wait_dma(self, eng, sem):
        self.wait(eng, sem, self.n[sem])

    def rendezvous(self):
        bar = {"t": "t", "v": "v", "s": "s", "g": "gb", "y": "yb"}
        def _fence_y(y):
            for s_ in ("y", "c0", "c1", "w0"):
                self.wait_dma(y, s_)
        self.y(_fence_y)
        def _fence_g(g):
            for s_ in ("g0", "g1", "gm"):
                self.wait_dma(g, s_)
        self.g(_fence_g)
        marks = {}
        for name, emit in (("t", self.t), ("v", self.v), ("s", self.s),
                           ("g", self.g), ("y", self.y)):
            def _inc(e, name=name):
                self.inc(e.drain(), bar[name])
            emit(_inc)
            marks[bar[name]] = self.n[bar[name]]
        for name, emit in (("t", self.t), ("v", self.v), ("s", self.s),
                           ("g", self.g), ("y", self.y)):
            def _w(e, name=name):
                for o, mv in marks.items():
                    if o != bar[name]:
                        e.wait_ge(self.h[o], mv)
            emit(_w)


def build_program(stop_after=None):
    nc = bass.Bass("TRN2", target_bir_lowering=False)

    def din(name, shape, dt=F32):
        return nc.dram_tensor(name, shape, dt, kind="ExternalInput").ap()

    x_in = din("x_sh", [TL, D])
    out = nc.dram_tensor("out", [TL, D], F32, kind="ExternalOutput").ap()

    wk0 = din("L0Wk", [D, D]); wv0 = din("L0Wv", [D, D])
    wr0 = din("L0Wr", [D, D]); wo0 = din("L0Wo", [D, D])
    wfk = din("L0Wfk", [D, H1]); wfv = din("L0Wfv", [H1, D])
    tdec = din("L0td", [D]); tfirst = din("L0tf", [D])
    ln1w = din("L0ln1w", [D]); ln1b = din("L0ln1b", [D])
    ln2w = din("L0ln2w", [D]); ln2b = din("L0ln2b", [D])
    wgu_sl = din("wgu_sl", [D // 8, 2 * H1])
    wp_sl = din("wp_sl", [H1 // 8, D])
    wq1 = din("L1Wq", [D, D // 2]); wk1 = din("L1Wk", [D, D // 2])
    wv1 = din("L1Wv", [D, D]); wg1 = din("L1Wg", [D, D])
    wgk1 = din("L1Wgk1", [D, 16]); wgk2 = din("L1Wgk2", [16, D // 2])
    bgk2 = din("L1bgk2", [D // 2]); normw = din("L1normw", [128])
    wo1 = din("L1Wo", [D, D])
    w0t = din("W0T", [D, H1]); b0 = din("b0", [H1])
    w1t = din("W1T", [H1, D]); b1 = din("b1", [D])
    ident = din("ident", [128, 128], BF16)
    onesblk = din("onesblk", [128, 2], BF16)
    masktri = din("masktri", [128, 128], BF16)
    cmask = din("cmask", [128, 8])
    mask0 = din("mask0", [128, 1])

    _loc = _os.environ.get("SIM_LOCAL") == "1"

    def dint(name, shape, dt=F32, shared=False):
        return nc.dram_tensor(name, shape, dt,
                              addr_space="Shared" if (shared and not _loc)
                              else "Local").ap()

    b_rwkv_i = dint("b_rwkv_i", [128, FC])
    b_rwkv_o = dint("b_rwkv_o", [128 * NCORES, FC], shared=True)
    b_gla_i = dint("b_gla_i", [128, 1032])
    b_gla_o = dint("b_gla_o", [128 * NCORES, 1032], shared=True)
    b_wgu_i = dint("b_wgu_i", [D // 8, 2 * H1], BF16)
    whgu = dint("whgu", [D, 2 * H1], BF16, shared=True)
    b_wp_i = dint("b_wp_i", [H1 // 8, D], BF16)
    whp = dint("whp", [H1, D], BF16, shared=True)
    s_gu_d = dint("s_gu_d", [4, 2 * H1])
    s_wp_d = dint("s_wp_d", [16, D])
    wfk_bf = dint("wfk_bf", [D, H1], BF16)
    w0t_bf = dint("w0t_bf", [D, H1], BF16)
    wfv_bf = dint("wfv_bf", [H1, D], BF16)
    w1t_bf = dint("w1t_bf", [H1, D], BF16)
    wv1_bf = dint("wv1_bf", [D, D], BF16)
    wg1_bf = dint("wg1_bf", [D, D], BF16)
    wo1_bf = dint("wo1_bf", [D, D], BF16)
    wo0_bf = dint("wo0_bf", [D, D], BF16)

    es = ExitStack()
    with es:
        sb = lambda n_, s_, dt=F32: es.enter_context(nc.sbuf_tensor(n_, s_, dt))
        x_tm = sb("x_tm", [128, NS * D])
        xT = sb("xT", [128, FC * TL], BF16)
        big_a = sb("big_a", [128, FC * TL], BF16)
        big_b = sb("big_b", [128, FC * TL], BF16)
        wslab = sb("wslab", [128, 2 * FC * 128], BF16)
        stripe_a = sb("stripe_a", [128, D])
        stripe_c = sb("stripe_c", [128, D], BF16)
        scr_f = sb("scr_f", [128, 2 * TL])
        id_sb = sb("id_sb", [128, 128], BF16)
        masktri_sb = sb("masktri_sb", [128, 128], BF16)
        cmask_sb = sb("cmask_sb", [128, 8])
        mask0_sb = sb("mask0_sb", [128, 1])
        ones1 = sb("ones1", [128, 1])
        onesblk_sb = sb("onesblk_sb", [128, 2], BF16)
        normw_b = sb("normw_b", [128, D], BF16)
        b0_sb = sb("b0_sb", [128, 64])
        vecs = sb("vecs", [128, FC * 8])
        st = {}
        for nm in ("sum", "sq", "m0", "i0", "sd0", "m1", "i1", "tmp", "tmp2",
                   "Mp", "Sp", "minv", "d10a", "d10b"):
            st[nm] = sb("st_" + nm, [128, 8])
        eloc = sb("eloc", [128, FC])
        Prw = sb("Prw", [128, FC])
        ebuf = sb("ebuf", [128, FC])
        v0c = sb("v0c", [128, FC])
        bonus = sb("bonus", [128, FC])
        gla_t = sb("gla_t", [128, 4096], BF16)
        S_run = sb("S_run", [128, 1024])
        ones_bf = sb("ones_bf", [128, 1], BF16)
        _CVALS = [EPS_LN, EPS_RMS, 1.0, -LN8]
        cons = sb("cons", [128, len(_CVALS)])

        def cb(val):
            i = _CVALS.index(val)
            return cons[:, i:i + 1]
        PBIG = es.enter_context(nc.psum_tensor("PBIG", [128, 2048], F32))
        PF = es.enter_context(nc.psum_tensor("PF", [128, 1024], F32))
        PT = es.enter_context(nc.psum_tensor("PT", [128, 1024], BF16))
        PX = es.enter_context(nc.psum_tensor("PX", [128, 512], F32))
        PTF = PT.bitcast(F32)

        bias1_b = gla_t[:, 0:D]
        sqrow = stripe_c[:].bitcast(F32)[0:1, 0:256]
        z1_sb = gla_t[:, 1024:1536].bitcast(F32).bitcast(BF16)[0:16, 0:1024] \
            if False else gla_t[0:16, 1024:2048]
        wgk2_sb = gla_t[0:16, 3072:4096]
        sst_f = scr_f[0:2, 512:1024]
        sst_b = scr_f[0:2, 1024:1536].bitcast(BF16)

        block = es.enter_context(nc.Block())
        mk = MK(nc, block, es)

        def ph0(y):
            for s_ in range(NS):
                mk.inc(y.dma_start(out=x_tm[:, s_ * D:(s_ + 1) * D],
                                   in_=x_in[s_ * 128:(s_ + 1) * 128, :]), "y", 16)
            for dst, src in ((id_sb, ident), (masktri_sb, masktri),
                             (cmask_sb, cmask), (mask0_sb, mask0),
                             (onesblk_sb, onesblk)):
                mk.inc(y.dma_start(out=dst[:], in_=src), "y", 16)
            for slot, vec in ((0, tdec), (1, tfirst), (2, ln1w), (3, ln1b),
                              (4, ln2w), (5, ln2b)):
                mk.inc(y.dma_start(
                    out=vecs[:, slot * FC:(slot + 1) * FC],
                    in_=vec.rearrange("(f p) -> p f", p=128),
                    allow_slow_non_contiguous=True), "y", 16)
            mk.inc(y.dma_start(
                out=vecs[:, 6 * FC:6 * FC + 8],
                in_=bgk2.rearrange("(f p) -> p f", p=128),
                allow_slow_non_contiguous=True), "y", 16)
            mk.inc(y.dma_start(
                out=b0_sb[:], in_=b0.rearrange("(f p) -> p f", p=128),
                allow_slow_non_contiguous=True), "y", 16)
        mk.y(ph0)
        y0 = mk.n["y"]

        def ph0g(g):
            mk.inc(g.dma_start(
                out=normw_b[:],
                in_=normw.unsqueeze(0).unsqueeze(0)
                .to_broadcast([128, FC, 128])), "gm", 16)
        mk.g(ph0g)

        def ph0v(v):
            mk.wait_dma(v, "y")
            for i, val in enumerate(_CVALS):
                v.memset(cons[:, i:i + 1], val)
            v.memset(ones_bf[:], 1.0)
            mk.inc(v.memset(ones1[:], 1.0), "v")
        mk.v(ph0v)

        def ph0s(s):
            mk.wait_dma(s, "y")
            s.activation(vecs[:, 0:FC], vecs[:, 0:FC], AF.Exp)
            s.drain()
            s.activation(vecs[:, 0:FC], vecs[:, 0:FC], AF.Exp, scale=-1.0)
            s.drain()
            mk.inc(s.activation(vecs[:, FC:2 * FC], vecs[:, FC:2 * FC],
                                AF.Exp), "s")
        mk.s(ph0s)
        conv_list = [(wfv, wfv_bf, H1, D), (w1t, w1t_bf, H1, D),
                     (wfk, wfk_bf, D, H1), (w0t, w0t_bf, D, H1),
                     (wv1, wv1_bf, D, D), (wg1, wg1_bf, D, D),
                     (wo1, wo1_bf, D, D), (wo0, wo0_bf, D, D)]
        ci = 0
        for w_ap, w_bf, rows, cols in conv_list:
            for r0 in range(0, rows, 128):
                for c0 in range(0, cols, D):
                    b = ci % 2
                    ci += 1
                    def _g(g):
                        mk.wait_dma(g, "c0" if b == 0 else "c1")
                        mk.inc(g.dma_start(
                            out=wslab[:, b * D:(b + 1) * D],
                            in_=w_ap[r0:r0 + 128, c0:c0 + D]),
                            "g0" if b == 0 else "g1", 16)
                    mk.g(_g)
                    def _y(y):
                        mk.wait_dma(y, "g0" if b == 0 else "g1")
                        mk.inc(y.dma_start(
                            out=w_bf[r0:r0 + 128, c0:c0 + D],
                            in_=wslab[:, b * D:(b + 1) * D]),
                            "c0" if b == 0 else "c1", 16)
                    mk.y(_y)
        mk.rendezvous()

        def lnstats(x_fn, n_str=NS, wait_v=0, wait_s=0):
            dumps = (stripe_a, scr_f)
            vred = [0, 0]
            for s_ in range(n_str):
                def _s(s):
                    if s_ == 0:
                        mk.wait(s, "v", wait_v)
                    mk.wait(s, "v", vred[s_ % 2])
                    mk.inc(s.activation(dumps[s_ % 2][:, 0:D], x_fn(s_),
                                        AF.Square), "s")
                mk.s(_s)
                sm_ = mk.n["s"]
                def _v(v):
                    if s_ == 0:
                        mk.wait(v, "s", wait_s)
                    v.tensor_reduce(st["sum"][:, s_:s_ + 1], x_fn(s_),
                                    AX.X, ALU.add)
                    mk.wait(v, "s", sm_)
                    mk.inc(v.tensor_reduce(st["sq"][:, s_:s_ + 1],
                                           dumps[s_ % 2][:, 0:D], AX.X,
                                           ALU.add), "v")
                mk.v(_v)
                vred[s_ % 2] = mk.n["v"]
            return mk.n["v"], mk.n["s"]

        def lnminv(m_t, i_t, sd_t=None, nfeat=D, eps=EPS_LN, n_str=NS,
                   wait_v=0, wait_s=0):
            c = 1.0 / nfeat
            def _v(v):
                mk.wait(v, "v", wait_v)
                mk.wait(v, "s", wait_s)
                v.drain()
                v.tensor_scalar(m_t[:, 0:n_str], st["sum"][:, 0:n_str], c,
                                None, ALU.mult)
                v.tensor_scalar(st["tmp"][:, 0:n_str], st["sq"][:, 0:n_str], c,
                                None, ALU.mult)
                v.drain()
                v.tensor_mul(st["tmp2"][:, 0:n_str], m_t[:, 0:n_str],
                             m_t[:, 0:n_str])
                v.drain()
                mk.inc(v.tensor_sub(st["tmp"][:, 0:n_str],
                                    st["tmp"][:, 0:n_str],
                                    st["tmp2"][:, 0:n_str]), "v")
            mk.v(_v)
            vm = mk.n["v"]
            def _s(s):
                mk.wait(s, "v", vm)
                s.activation(st["tmp2"][:, 0:n_str], st["tmp"][:, 0:n_str],
                             AF.Ln, bias=cb(eps))
                s.drain()
                s.activation(i_t[:, 0:n_str], st["tmp2"][:, 0:n_str],
                             AF.Exp, scale=-0.5)
                if sd_t is not None:
                    s.activation(sd_t[:, 0:n_str], st["tmp2"][:, 0:n_str],
                                 AF.Exp, scale=0.5)
                mk.inc(s.drain(), "s")
            mk.s(_s)
            sm = mk.n["s"]
            def _v2(v):
                mk.wait(v, "s", sm)
                mk.inc(v.drain(), "v")
            mk.v(_v2)
            return mk.n["v"]

        pt_guard = [0]

        def transpose_stripe(src_fn, s_idx, dst_sb, gw=None, gb=None,
                             wait_t=()):
            for c0 in (0, 8):
                def _t(t):
                    for sem, val in wait_t:
                        mk.wait(t, sem, val)
                    mk.wait(t, "s", pt_guard[0])
                    for j in range(8):
                        ins = t.transpose(PT[:, j * 128:(j + 1) * 128],
                                          src_fn(c0 + j), id_sb[:])
                    mk.inc(ins, "t")
                mk.t(_t)
                tm = mk.n["t"]
                def _s(s):
                    mk.wait(s, "t", tm)
                    for j in range(8):
                        c = c0 + j
                        dcol = c * TL + s_idx * 128
                        if gw is not None:
                            ins = s.activation(dst_sb[:, dcol:dcol + 128],
                                               PT[:, j * 128:(j + 1) * 128],
                                               AF.Identity, scale=gw(c), bias=gb(c))
                        else:
                            ins = s.copy(dst_sb[:, dcol:dcol + 128],
                                         PT[:, j * 128:(j + 1) * 128])
                    mk.inc(ins, "s")
                mk.s(_s)
                pt_guard[0] = mk.n["s"]

        slab_guard = [0, 0]

        def load_kslab(w_ap, of0, ncols, buf, krows=D, dram_bf=False):
            if dram_bf:
                sem = "w0" if buf == 0 else "g1"
                def _do(e):
                    mk.wait(e, "t", slab_guard[buf])
                    ins = e.dma_start(
                        out=wslab[:, buf * D:buf * D + (krows // 128) * ncols],
                        in_=w_ap[0:krows, of0:of0 + ncols]
                        .rearrange("(k p) n -> p k n", p=128))
                    mk.inc(ins, sem, 16)
                if buf == 0:
                    mk.y(_do)
                else:
                    mk.g(_do)
                return (sem, mk.n[sem])
            sem = "g0" if buf == 0 else "g1"
            def _g(g):
                mk.wait(g, "t", slab_guard[buf])
                ins = g.dma_start(
                    out=wslab[:, buf * D:buf * D + (krows // 128) * ncols],
                    in_=w_ap[0:krows, of0:of0 + ncols]
                    .rearrange("(k p) n -> p k n", p=128))
                mk.inc(ins, sem, 16)
            mk.g(_g)
            return (sem, mk.n[sem])

        def load_rowslab(w_ap, row0, ncols, buf, dram_bf=False):
            if dram_bf:
                sem = "w0" if buf == 0 else "g1"
                def _do(e):
                    mk.wait(e, "t", slab_guard[buf])
                    ins = e.dma_start(
                        out=wslab[:, buf * D:buf * D + ncols],
                        in_=w_ap[row0:row0 + 128, 0:ncols])
                    mk.inc(ins, sem, 16)
                if buf == 0:
                    mk.y(_do)
                else:
                    mk.g(_do)
                return (sem, mk.n[sem])
            sem = "g0" if buf == 0 else "g1"
            def _do(e):
                mk.wait(e, "t", slab_guard[buf])
                ins = e.dma_start(
                    out=wslab[:, buf * D:buf * D + ncols],
                    in_=w_ap[row0:row0 + 128, 0:ncols])
                mk.inc(ins, sem, 16)
            mk.g(_do)
            return (sem, mk.n[sem])

        def ln_post_stripe(s_, vbo):
            def _sq(s):
                mk.wait(s, "v", vbo)
                mk.inc(s.activation(scr_f[:, 0:D], stripe_a[:], AF.Square),
                       "s")
            mk.s(_sq)
            sqm = mk.n["s"]
            def _v1(v):
                mk.wait(v, "s", sqm)
                v.tensor_reduce(st["sum"][:, 0:1], stripe_a[:], AX.X, ALU.add)
                v.tensor_reduce(st["sq"][:, 0:1], scr_f[:, 0:D], AX.X, ALU.add)
                c_ = 1.0 / D
                v.drain()
                v.tensor_scalar(st["tmp"][:, 0:1], st["sum"][:, 0:1], c_,
                                None, ALU.mult)
                v.tensor_scalar(st["tmp2"][:, 0:1], st["sq"][:, 0:1], c_,
                                None, ALU.mult)
                v.drain()
                v.tensor_mul(st["minv"][:, 0:1], st["tmp"][:, 0:1],
                             st["tmp"][:, 0:1])
                v.drain()
                mk.inc(v.tensor_sub(st["tmp2"][:, 0:1], st["tmp2"][:, 0:1],
                                    st["minv"][:, 0:1]), "v")
            mk.v(_v1)
            vm_ = mk.n["v"]
            def _sm(s):
                mk.wait(s, "v", vm_)
                s.activation(st["tmp2"][:, 0:1], st["tmp2"][:, 0:1],
                             AF.Ln, bias=cb(EPS_LN))
                s.drain()
                mk.inc(s.activation(st["tmp2"][:, 0:1], st["tmp2"][:, 0:1],
                                    AF.Exp, scale=-0.5), "s")
            mk.s(_sm)
            sm_ = mk.n["s"]
            def _vr(v):
                mk.wait(v, "s", sm_)
                v.tensor_mul(st["minv"][:, 0:1], st["tmp"][:, 0:1],
                             st["tmp2"][:, 0:1])
                v.drain()
                v.tensor_scalar(x_tm[:, s_ * D:(s_ + 1) * D],
                                x_tm[:, s_ * D:(s_ + 1) * D],
                                st["minv"][:, 0:1], None, ALU.subtract)
                v.drain()
                mk.inc(v.scalar_tensor_tensor(
                    x_tm[:, s_ * D:(s_ + 1) * D], stripe_a[:],
                    st["tmp2"][:, 0:1], x_tm[:, s_ * D:(s_ + 1) * D],
                    ALU.mult, ALU.add), "v")
            mk.v(_vr)
            return mk.n["v"]

        qsl = stripe_c
        thrB = scr_f

        def quant_slice(w_sl, nrc, ncc, csz, s_dram, bounce):
            for r in range(nrc):
                for cc in range(ncc):
                    def _g(g):
                        mk.wait(g, "s", mk.n["s"])
                        ins = g.dma_start(
                            out=qsl[:, 0:csz],
                            in_=w_sl[r * 128:(r + 1) * 128,
                                     cc * csz:(cc + 1) * csz])
                        mk.inc(ins, "gm", 16)
                    mk.g(_g)
                    def _s(s):
                        mk.wait_dma(s, "gm")
                        mk.wait(s, "t", mk.n["t"])
                        mk.wait(s, "v", mk.n["v"])
                        mk.inc(s.activation(qsl[:, csz:2 * csz], qsl[:, 0:csz],
                                            AF.Abs), "s")
                    mk.s(_s)
                    sm = mk.n["s"]
                    def _t(t):
                        mk.wait(t, "s", sm)
                        mk.wait(t, "v", mk.n["v"])
                        mk.inc(t.matmul(PX[0:2, 0:csz], onesblk_sb[:],
                                        qsl[:, csz:2 * csz],
                                        start=True, stop=True), "t")
                    mk.t(_t)
                    tm = mk.n["t"]
                    def _v(v):
                        mk.wait(v, "t", tm)
                        mk.wait_dma(v, "y")
                        v.tensor_scalar(sst_b[:, 0:csz], PX[0:2, 0:csz],
                                        1.0 / 64, 1e-8, ALU.mult, ALU.max)
                        v.drain()
                        mk.inc(v.tensor_scalar(sst_f[:, 0:csz], sst_b[:, 0:csz],
                                               THR_C, None, ALU.mult), "v")
                    mk.v(_v)
                    vm = mk.n["v"]
                    def _y(y):
                        mk.wait(y, "v", vm)
                        mk.inc(y.dma_start(
                            out=s_dram[2 * r:2 * r + 2, cc * csz:(cc + 1) * csz],
                            in_=sst_f[:, 0:csz]), "y", 16)
                    mk.y(_y)
            for r in range(nrc):
                for cc in range(ncc):
                    def _y2(y):
                        mk.wait(y, "v", mk.n["v"])
                        ins = y.dma_start(
                            out=thrB[:, 0:csz],
                            in_=s_dram[2 * r:2 * r + 2, cc * csz:(cc + 1) * csz]
                            .unsqueeze(1).to_broadcast([2, 64, csz]))
                        mk.inc(ins, "y", 16)
                    mk.y(_y2)
                    def _g2(g):
                        mk.wait(g, "s", mk.n["s"])
                        ins = g.dma_start(
                            out=qsl[:, 0:csz],
                            in_=w_sl[r * 128:(r + 1) * 128,
                                     cc * csz:(cc + 1) * csz])
                        mk.inc(ins, "gm", 16)
                    mk.g(_g2)
                    def _s2(s):
                        mk.wait_dma(s, "gm")
                        mk.wait(s, "v", mk.n["v"])
                        s.activation(qsl[:, csz:2 * csz], qsl[:, 0:csz], AF.Abs)
                        mk.inc(s.sign(qsl[:, 2 * csz:3 * csz], qsl[:, 0:csz]),
                               "s")
                    mk.s(_s2)
                    sm2 = mk.n["s"]
                    def _v2(v):
                        mk.wait(v, "s", sm2)
                        mk.wait_dma(v, "y")
                        v.tensor_tensor(qsl[:, csz:2 * csz],
                                        qsl[:, csz:2 * csz],
                                        thrB[:, 0:csz], ALU.is_ge)
                        v.drain()
                        v.tensor_mul(qsl[:, csz:2 * csz], qsl[:, csz:2 * csz],
                                     qsl[:, 2 * csz:3 * csz])
                        v.drain()
                        mk.inc(v.scalar_tensor_tensor(
                            qsl[:, 3 * csz:4 * csz], thrB[:, 0:csz],
                            1.0 / THR_C, qsl[:, csz:2 * csz],
                            ALU.mult, ALU.mult), "v")
                    mk.v(_v2)
                    vm2 = mk.n["v"]
                    def _y3(y):
                        mk.wait(y, "v", vm2)
                        mk.inc(y.dma_start(
                            out=bounce[r * 128:(r + 1) * 128,
                                       cc * csz:(cc + 1) * csz],
                            in_=qsl[:, 3 * csz:4 * csz]), "y", 16)
                    mk.y(_y3)

        quant_slice(wgu_sl, 2, 32, 512, s_gu_d, b_wgu_i)
        quant_slice(wp_sl, 8, 4, 512, s_wp_d, b_wp_i)
        yq = mk.n["y"]

        SKIP_CC = _os.environ.get("SKIP_CC") == "1"
        SKIP_CC_BIG = SKIP_CC or _os.environ.get("SKIP_CC_BIG") == "1"
        def _gq(g):
            mk.wait_dma(g, "y")
            if SKIP_CC_BIG:
                for j in range(NCORES):
                    mk.inc(g.dma_start(
                        out=whgu[j * (D // 8):(j + 1) * (D // 8), :],
                        in_=b_wgu_i[:, :]), "gm", 16)
                    mk.inc(g.dma_start(
                        out=whp[j * (H1 // 8):(j + 1) * (H1 // 8), :],
                        in_=b_wp_i[:, :]), "gm", 16)
                g.wait_ge(mk.h["gm"], mk.n["gm"])
                mk.inc(g.drain(), "c")
                mk.inc(g.drain(), "c")
                return
            mk.inc(g.collective_compute(
                "AllGather", ALU.bypass, replica_groups=[list(range(NCORES))],
                ins=[b_wgu_i.opt()], outs=[whgu.opt()]), "c")
            mk.inc(g.collective_compute(
                "AllGather", ALU.bypass, replica_groups=[list(range(NCORES))],
                ins=[b_wp_i.opt()], outs=[whp.opt()]), "c")
        mk.g(_gq)
        c_whq = mk.n["c"]
        mk.rendezvous()

        vm, sm = lnstats(lambda s_: x_tm[:, s_ * D:(s_ + 1) * D])
        v_ln0 = lnminv(st["m0"], st["i0"], sd_t=st["sd0"], wait_v=vm, wait_s=sm)
        if stop_after == "DBG00":
            def _yd(y):
                mk.wait(y, "v", v_ln0)
                for i, nm in enumerate(("m0", "i0", "sum", "sq", "sd0",
                                        "tmp", "tmp2")):
                    mk.inc(y.dma_start(out=out[0:128, i * 8:(i + 1) * 8],
                                       in_=st[nm][:, 0:8]), "y", 16)
                mk.inc(y.dma_start(out=out[0:128, 56:60],
                                   in_=cons[:, 0:4]), "y", 16)
                mk.wait(y, "y", mk.n["y"])
            mk.y(_yd)
            mk.rendezvous()
            return nc
        sq_s = [0]
        for s_ in range(NS):
            def _vx(v):
                mk.wait(v, "s", sq_s[0])
                v.tensor_scalar(stripe_a[:], x_tm[:, s_ * D:(s_ + 1) * D],
                                st["m0"][:, s_:s_ + 1], st["i0"][:, s_:s_ + 1],
                                ALU.subtract, ALU.mult)
                v.drain()
                mk.inc(v.tensor_reduce(st["sum"][:, s_:s_ + 1], stripe_a[:],
                                       AX.X, ALU.add), "v")
            mk.v(_vx)
            vxm = mk.n["v"]
            def _sx(s):
                mk.wait(s, "v", vxm)
                mk.inc(s.activation(scr_f[:, 0:D], stripe_a[:], AF.Square), "s")
            mk.s(_sx)
            sq_s[0] = mk.n["s"]
            sxm = mk.n["s"]
            def _vx2(v):
                mk.wait(v, "s", sxm)
                mk.inc(v.tensor_reduce(st["sq"][:, s_:s_ + 1], scr_f[:, 0:D],
                                       AX.X, ALU.add), "v")
            mk.v(_vx2)
        vm = mk.n["v"]
        lnminv(st["m1"], st["i1"], wait_v=vm)
        def _vc(v):
            v.drain()
            v.tensor_mul(st["Mp"][:], st["m1"][:], st["sd0"][:])
            v.drain()
            v.tensor_add(st["Mp"][:], st["Mp"][:], st["m0"][:])
            mk.inc(v.tensor_mul(st["Sp"][:], st["i0"][:], st["i1"][:]), "v")
        mk.v(_vc)
        vcomp = mk.n["v"]
        if stop_after == "DBG0":
            def _yd(y):
                mk.wait(y, "v", vcomp)
                for i, nm in enumerate(("m0", "i0", "m1", "i1", "Mp", "Sp",
                                        "sum", "sq", "sd0")):
                    mk.inc(y.dma_start(out=out[0:128, i * 8:(i + 1) * 8],
                                       in_=st[nm][:, 0:8]), "y", 16)
                mk.wait(y, "y", mk.n["y"])
            mk.y(_yd)
            mk.rendezvous()
            return nc
        if stop_after == "DBG1":
            def _vd(v):
                mk.wait(v, "v", vcomp)
                for s_ in range(NS):
                    ins = v.tensor_scalar(x_tm[:, s_ * D:(s_ + 1) * D],
                                          x_tm[:, s_ * D:(s_ + 1) * D],
                                          st["Mp"][:, s_:s_ + 1],
                                          st["Sp"][:, s_:s_ + 1],
                                          ALU.subtract, ALU.mult)
                mk.inc(ins, "v")
            mk.v(_vd)
            mk.rendezvous()
            _finish(mk, out, x_tm)
            return nc
        for s_ in range(NS):
            def _va(v):
                if s_ == 0:
                    mk.wait(v, "v", vcomp)
                    v.drain()
                mk.wait(v, "s", pt_guard[0])
                mk.inc(v.tensor_scalar(stripe_c[:],
                                       x_tm[:, s_ * D:(s_ + 1) * D],
                                       st["Mp"][:, s_:s_ + 1],
                                       st["Sp"][:, s_:s_ + 1],
                                       ALU.subtract, ALU.mult), "v")
            mk.v(_va)
            va = mk.n["v"]
            transpose_stripe(lambda c: stripe_c[:, c * 128:(c + 1) * 128], s_,
                             xT,
                             gw=lambda c: vecs[:, 2 * FC + c:2 * FC + c + 1],
                             gb=lambda c: vecs[:, 3 * FC + c:3 * FC + c + 1],
                             wait_t=(("v", va),))
        xT_done = mk.n["s"]

        kv_guard = [0]
        uv_guard = [0]
        for c in range(FC):
            gk = load_kslab(wk0, c * 128, 128, 0)
            gv = load_kslab(wv0, c * 128, 128, 1)
            def _t(t):
                mk.wait(t, gk[0], gk[1])
                mk.wait(t, gv[0], gv[1])
                if c == 0:
                    mk.wait(t, "s", xT_done)
                mk.wait(t, "s", kv_guard[0])
                for half in range(2):
                    for k in range(FC):
                        ins = t.matmul(
                            PF[:, half * 512:(half + 1) * 512],
                            wslab[:, k * 128:k * 128 + 128],
                            xT[:, k * TL + half * 512:k * TL + half * 512 + 512],
                            start=(k == 0), stop=(k == FC - 1))
                mk.inc(ins, "t")
                mk.wait(t, "v", uv_guard[0])
                for half in range(2):
                    for k in range(FC):
                        ins = t.matmul(
                            PBIG[:, half * 512:(half + 1) * 512],
                            wslab[:, D + k * 128:D + k * 128 + 128],
                            xT[:, k * TL + half * 512:k * TL + half * 512 + 512],
                            start=(k == 0), stop=(k == FC - 1))
                mk.inc(ins, "t")
            mk.t(_t)
            tk, tv = mk.n["t"] - 1, mk.n["t"]
            slab_guard[0] = tk
            slab_guard[1] = tv
            def _s(s):
                mk.wait(s, "t", tk)
                mk.wait(s, "v", mk.n["v"])
                mk.inc(s.activation(scr_f[:, 0:TL], PF[:, 0:TL], AF.Exp), "s")
            mk.s(_s)
            kv_guard[0] = mk.n["s"]
            se = mk.n["s"]
            def _v(v):
                mk.wait(v, "s", se)
                mk.wait(v, "t", tv)
                mk.inc(v.tensor_mul(big_a[:, c * TL:(c + 1) * TL],
                                    scr_f[:, 0:TL], PBIG[:, 0:TL]), "v")
                v.tensor_copy(v0c[:, c:c + 1], PBIG[:, 0:1])
                v.drain()
                v.tensor_tensor_scan(
                    scr_f[:, TL:2 * TL],
                    vecs[:, c:c + 1].to_broadcast([128, TL]),
                    big_a[:, c * TL:(c + 1) * TL],
                    0.0, ALU.mult, ALU.add)
                v.drain()
                mk.inc(v.tensor_copy(eloc[:, c:c + 1],
                                     scr_f[:, 2 * TL - 1:2 * TL]), "v")
            mk.v(_v)
            uv_guard[0] = mk.n["v"]
        v_scan = mk.n["v"]
        if stop_after == "DBG2":
            def _yd(y):
                mk.wait(y, "v", v_scan)
                mk.inc(y.dma_start(
                    out=out.rearrange("(g p) d -> p g d", p=128),
                    in_=big_a[:, 0:FC * TL].rearrange("p (g d) -> p g d", d=D)),
                    "y", 16)
                mk.wait(y, "y", mk.n["y"])
            def _gd(g):
                mk.wait(g, "v", v_scan)
                mk.inc(g.dma_start(
                    out=out.rearrange("(g p) d -> p g d", p=128),
                    in_=big_a[:, 0:FC * TL].rearrange("p (g d) -> p g d", d=D)),
                    "gm", 16)
                mk.wait_dma(g, "gm")
            mk.g(_gd)
            mk.rendezvous()
            return nc

        def _ye(y):
            mk.wait(y, "v", v_scan)
            mk.inc(y.dma_start(out=b_rwkv_i[:, :], in_=eloc[:]), "y", 16)
        mk.y(_ye)
        def _gc(g):
            mk.wait_dma(g, "y")
            if SKIP_CC:
                for j in range(NCORES):
                    mk.inc(g.dma_start(out=b_rwkv_o[j * 128:(j + 1) * 128, :],
                                       in_=b_rwkv_i[:, :]), "gm", 16)
                g.wait_ge(mk.h["gm"], mk.n["gm"])
                mk.inc(g.drain(), "c")
                return
            mk.inc(g.collective_compute(
                "AllGather", ALU.bypass, replica_groups=[list(range(NCORES))],
                ins=[b_rwkv_i.opt()], outs=[b_rwkv_o.opt()]), "c")
        mk.g(_gc)
        c_rw = mk.n["c"]
        def _sd(s):
            mk.wait(s, "v", mk.n["v"])
            s.activation(st["d10a"][:], vecs[:, 0:8], AF.Ln)
            s.activation(st["d10b"][:], vecs[:, 8:FC], AF.Ln)
            s.drain()
            s.activation(st["d10a"][:], st["d10a"][:], AF.Exp, scale=float(TL))
            mk.inc(s.activation(st["d10b"][:], st["d10b"][:], AF.Exp,
                                scale=float(TL)), "s")
        mk.s(_sd)
        s_d1024 = mk.n["s"]
        def _gl(g):
            mk.wait(g, "c", c_rw)
            mk.wait(g, "v", mk.n["v"])
            mk.inc(g.dma_start(
                out=scr_f[:, 0:7 * FC].rearrange("p (j f) -> p j f", f=FC),
                in_=b_rwkv_o[0:7 * 128, :]
                .rearrange("(j p) f -> p j f", p=128)), "gm", 16)
        mk.g(_gl)
        def _vp(v):
            mk.wait_dma(v, "gm")
            mk.wait(v, "s", s_d1024)
            v.memset(ebuf[:], 0.0)
            v.memset(Prw[:], 0.0)
            for j in range(7):
                v.drain()
                v.tensor_mul(ebuf[:, 0:8], ebuf[:, 0:8], st["d10a"][:])
                v.tensor_mul(ebuf[:, 8:FC], ebuf[:, 8:FC], st["d10b"][:])
                v.drain()
                v.tensor_add(ebuf[:], ebuf[:], scr_f[:, j * FC:(j + 1) * FC])
                v.drain()
                mk.inc(v.scalar_tensor_tensor(
                    Prw[:], ebuf[:], cmask_sb[:, j:j + 1], Prw[:],
                    ALU.mult, ALU.add), "v")
            v.drain()
            mk.inc(v.tensor_mul(bonus[:], vecs[:, FC:2 * FC], v0c[:]), "v")
        mk.v(_vp)
        v_prw = mk.n["v"]
        if stop_after == "DBG3":
            def _yd(y):
                mk.wait(y, "v", v_prw)
                mk.inc(y.dma_start(out=out[0:128, 0:FC], in_=eloc[:]), "y", 16)
                mk.inc(y.dma_start(out=out[0:128, 16:32], in_=Prw[:]), "y", 16)
                mk.inc(y.dma_start(out=out[0:128, 32:48], in_=ebuf[:]), "y", 16)
                mk.inc(y.dma_start(out=out[0:128, 48:160],
                                   in_=scr_f[:, 0:112]), "y", 16)
                mk.inc(y.dma_start(out=out[0:128, 160:176],
                                   in_=v0c[:]), "y", 16)
                mk.inc(y.dma_start(out=out[0:128, 176:184],
                                   in_=st["d10a"][:]), "y", 16)
                mk.inc(y.dma_start(out=out[0:128, 184:192],
                                   in_=st["d10b"][:]), "y", 16)
                mk.wait(y, "y", mk.n["y"])
            mk.y(_yd)
            mk.rendezvous()
            return nc

        rw_guard = [kv_guard[0]]
        for c in range(FC):
            gr = load_kslab(wr0, c * 128, 128, c % 2)
            def _t(t):
                mk.wait(t, gr[0], gr[1])
                mk.wait(t, "s", rw_guard[0])
                for half in range(2):
                    for k in range(FC):
                        ins = t.matmul(
                            PF[:, half * 512:(half + 1) * 512],
                            wslab[:, (c % 2) * D + k * 128:
                                  (c % 2) * D + k * 128 + 128],
                            xT[:, k * TL + half * 512:k * TL + half * 512 + 512],
                            start=(k == 0), stop=(k == FC - 1))
                mk.inc(ins, "t")
            mk.t(_t)
            tr = mk.n["t"]
            slab_guard[c % 2] = tr
            def _s(s):
                mk.wait(s, "t", tr)
                mk.wait(s, "v", mk.n["v"])
                mk.inc(s.activation(big_b[:, c * TL:(c + 1) * TL], PF[:, 0:TL],
                                    AF.Sigmoid), "s")
            mk.s(_s)
            rw_guard[0] = mk.n["s"]
            sr = mk.n["s"]
            def _v(v):
                mk.wait(v, "s", sr)
                if c == 0:
                    mk.wait(v, "v", v_prw)
                v.drain()
                v.tensor_tensor_scan(
                    scr_f[:, 0:TL],
                    vecs[:, c:c + 1].to_broadcast([128, TL]),
                    big_a[:, c * TL:(c + 1) * TL],
                    Prw[:, c:c + 1], ALU.mult, ALU.add)
                v.drain()
                v.scalar_tensor_tensor(
                    scr_f[:, 0:1], bonus[:, c:c + 1], mask0_sb[:, 0:1],
                    scr_f[:, 0:1], ALU.mult, ALU.add)
                v.drain()
                mk.inc(v.tensor_mul(big_a[:, c * TL:(c + 1) * TL],
                                    big_b[:, c * TL:(c + 1) * TL],
                                    scr_f[:, 0:TL]), "v")
            mk.v(_v)
        v_rwkv = mk.n["v"]
        if stop_after == "DBG35":
            def _gd(g):
                mk.wait(g, "v", v_rwkv)
                mk.inc(g.dma_start(
                    out=out.rearrange("(g p) d -> p g d", p=128),
                    in_=big_a[:, 0:FC * TL].rearrange("p (g d) -> p g d", d=D)),
                    "gm", 16)
                mk.wait_dma(g, "gm")
            mk.g(_gd)
            mk.rendezvous()
            return nc

        po_guard = [0]
        for s_ in range(NS):
            for k in range(FC):
                wm = load_rowslab(wo0_bf, k * 128, D, k % 2, dram_bf=True)
                def _t(t):
                    mk.wait(t, wm[0], wm[1])
                    if k == 0:
                        mk.wait(t, "v", v_rwkv)
                        mk.wait(t, "v", po_guard[0])
                    for ofg in range(4):
                        ins = t.matmul(
                            PBIG[:, ofg * 512:(ofg + 1) * 512],
                            big_a[:, k * TL + s_ * 128:k * TL + s_ * 128 + 128],
                            wslab[:, (k % 2) * D + ofg * 512:
                                  (k % 2) * D + ofg * 512 + 512],
                            start=(k == 0), stop=(k == FC - 1))
                    mk.inc(ins, "t")
                mk.t(_t)
                slab_guard[k % 2] = mk.n["t"]
            tm = mk.n["t"]
            def _v(v):
                mk.wait(v, "t", tm)
                for ofg in range(4):
                    v.tensor_scalar(scr_f[:, ofg * 512:ofg * 512 + 512],
                                    x_tm[:, s_ * D + ofg * 512:
                                         s_ * D + ofg * 512 + 512],
                                    st["m0"][:, s_:s_ + 1],
                                    st["i0"][:, s_:s_ + 1],
                                    ALU.subtract, ALU.mult)
                v.drain()
                for ofg in range(4):
                    ins = v.tensor_add(
                        big_b[:, s_ * D + ofg * 512:s_ * D + ofg * 512 + 512],
                        scr_f[:, ofg * 512:ofg * 512 + 512],
                        PBIG[:, ofg * 512:(ofg + 1) * 512])
                mk.inc(ins, "v")
            mk.v(_v)
            po_guard[0] = mk.n["v"]
            if stop_after == "DBG37" and s_ == int(_os.environ.get("DBG_STRIPE", "0")):
                def _sd2(s):
                    mk.wait(s, "t", tm)
                    mk.wait(s, "v", po_guard[0])
                    mk.inc(s.copy(stripe_a[:], PBIG[:, 0:D]), "s")
                mk.s(_sd2)
                sdm = mk.n["s"]
                def _yd(y):
                    mk.wait(y, "s", sdm)
                    mk.inc(y.dma_start(out=out[0:128, :], in_=stripe_a[:]),
                           "y", 16)
                    mk.wait(y, "v", po_guard[0])
                    mk.inc(y.dma_start(out=out[128:256, 0:512],
                                       in_=scr_f[:, 0:512]), "y", 16)
                    mk.wait(y, "y", mk.n["y"])
                mk.y(_yd)
                def _gd(g):
                    mk.wait(g, "v", po_guard[0])
                    mk.inc(g.dma_start(out=out[256:384, :],
                                       in_=wslab[:, D:2 * D]), "gm", 16)
                    for q in range(4):
                        mk.inc(g.dma_start(
                            out=out[384 + q * 128:512 + q * 128, :],
                            in_=big_b[:, q * D:(q + 1) * D]), "gm", 16)
                    mk.wait_dma(g, "gm")
                mk.g(_gd)
                mk.rendezvous()
                return nc
        def _vdr(v):
            mk.inc(v.drain(), "v")
        mk.v(_vdr)
        v_x1 = mk.n["v"]
        if stop_after == "DBG4":
            def _gd(g):
                mk.wait(g, "v", v_x1)
                for s_ in range(NS):
                    ins = g.dma_start(out=out[s_ * 128:(s_ + 1) * 128, :],
                                      in_=big_b[:, s_ * D:(s_ + 1) * D])
                    mk.inc(ins, "gm", 16)
                mk.wait_dma(g, "gm")
            mk.g(_gd)
            mk.rendezvous()
            return nc

        vm, sm = lnstats(lambda s_: big_b[:, s_ * D:(s_ + 1) * D],
                         wait_v=v_x1, wait_s=0)
        v_ln2 = lnminv(st["m1"], st["i1"], wait_v=vm, wait_s=sm)
        hid = big_a
        for gi in range(4):
            for si in range(2):
                s_ = gi * 2 + si
                def _va(v):
                    if gi == 0 and si == 0:
                        mk.wait(v, "v", v_ln2)
                    mk.wait(v, "s", pt_guard[0])
                    mk.inc(v.tensor_scalar(stripe_c[:],
                                           big_b[:, s_ * D:(s_ + 1) * D],
                                           st["m1"][:, s_:s_ + 1],
                                           st["i1"][:, s_:s_ + 1],
                                           ALU.subtract, ALU.mult), "v")
                mk.v(_va)
                va = mk.n["v"]
                transpose_stripe(lambda c: stripe_c[:, c * 128:(c + 1) * 128],
                                 si, xT,
                                 gw=lambda c: vecs[:, 4 * FC + c:4 * FC + c + 1],
                                 gb=lambda c: vecs[:, 5 * FC + c:5 * FC + c + 1],
                                 wait_t=(("v", va),))
            xl2T_done = mk.n["s"]
            pf_guard2 = [0, 0]
            for c2 in range(64):
                wm = load_kslab(wfk_bf, c2 * 128, 128, c2 % 2, dram_bf=True)
                def _t(t):
                    mk.wait(t, wm[0], wm[1])
                    if c2 == 0:
                        mk.wait(t, "s", xl2T_done)
                        mk.wait(t, "v", mk.n["v"])
                    mk.wait(t, "s", pf_guard2[c2 % 2])
                    b = c2 % 2
                    for k in range(FC):
                        ins = t.matmul(
                            PF[:, b * 512:b * 512 + 256],
                            wslab[:, b * D + k * 128:b * D + k * 128 + 128],
                            xT[:, k * TL:k * TL + 256],
                            start=(k == 0), stop=(k == FC - 1))
                    mk.inc(ins, "t")
                mk.t(_t)
                tm2 = mk.n["t"]
                slab_guard[c2 % 2] = tm2
                def _s(s):
                    mk.wait(s, "t", tm2)
                    b = c2 % 2
                    mk.wait(s, "v", mk.n["v"])
                    mk.inc(s.activation(stripe_a[:, b * 256:b * 256 + 256],
                                        PF[:, b * 512:b * 512 + 256],
                                        AF.Relu), "s")
                mk.s(_s)
                pf_guard2[c2 % 2] = mk.n["s"]
                srl = mk.n["s"]
                def _vsq(v):
                    mk.wait(v, "s", srl)
                    b = c2 % 2
                    mk.inc(v.tensor_mul(hid[:, c2 * 256:(c2 + 1) * 256],
                                        stripe_a[:, b * 256:b * 256 + 256],
                                        stripe_a[:, b * 256:b * 256 + 256]),
                           "v")
                mk.v(_vsq)
            hid_done = ("v", mk.n["v"])
            sec1 = [PF[:, 0:512], PF[:, 512:1024], PX[:, 0:512],
                    PTF[:, 0:512]]
            for k2 in range(64):
                wm = load_rowslab(wfv_bf, k2 * 128, D, k2 % 2, dram_bf=True)
                def _t(t):
                    mk.wait(t, wm[0], wm[1])
                    if k2 == 0:
                        mk.wait(t, hid_done[0], hid_done[1])
                        mk.wait(t, "v", mk.n["v"])
                        mk.wait(t, "s", mk.n["s"])
                    for si in range(2):
                        for ofg in range(4):
                            dst = (PBIG[:, ofg * 512:(ofg + 1) * 512]
                                   if si == 0 else sec1[ofg])
                            ins = t.matmul(
                                dst,
                                hid[:, k2 * 256 + si * 128:
                                    k2 * 256 + si * 128 + 128],
                                wslab[:, (k2 % 2) * D + ofg * 512:
                                      (k2 % 2) * D + ofg * 512 + 512],
                                start=(k2 == 0), stop=(k2 == 63))
                    mk.inc(ins, "t")
                mk.t(_t)
                slab_guard[k2 % 2] = mk.n["t"]
            tm3 = mk.n["t"]
            for si in range(2):
                s_ = gi * 2 + si
                def _v(v):
                    if si == 0:
                        mk.wait(v, "t", tm3)
                        v.tensor_add(stripe_a[:],
                                     big_b[:, s_ * D:(s_ + 1) * D],
                                     PBIG[:, 0:D])
                    else:
                        v.drain()
                        for ofg in range(4):
                            v.tensor_add(
                                stripe_a[:, ofg * 512:(ofg + 1) * 512],
                                big_b[:, s_ * D + ofg * 512:
                                      s_ * D + ofg * 512 + 512],
                                sec1[ofg])
                    v.drain()
                    mk.inc(v.tensor_reduce(st["sum"][:, 0:1], stripe_a[:],
                                           AX.X, ALU.add), "v")
                mk.v(_v)
                vbo = mk.n["v"]
                if stop_after == "DBG5" and gi == 0 and si == 0:
                    def _yd(y):
                        mk.wait(y, "v", vbo)
                        mk.inc(y.dma_start(out=out[0:128, :],
                                           in_=stripe_a[:]), "y", 16)
                        mk.wait(y, "y", mk.n["y"])
                    mk.y(_yd)
                    def _gd(g):
                        mk.wait(g, "v", vbo)
                        mk.inc(g.dma_start(
                            out=out[128:1024, 0:2048]
                            .rearrange("(a p) d -> p a d", p=128),
                            in_=big_a[:, 0:7 * D]
                            .rearrange("p (a d) -> p a d", d=D)), "gm", 16)
                        mk.wait_dma(g, "gm")
                    mk.g(_gd)
                    mk.rendezvous()
                    return nc
                def _sq(s):
                    mk.wait(s, "v", vbo)
                    mk.inc(s.activation(scr_f[:, 0:D], stripe_a[:],
                                        AF.Square), "s")
                mk.s(_sq)
                sqm = mk.n["s"]
                def _v1b(v):
                    mk.wait(v, "s", sqm)
                    v.tensor_reduce(st["sq"][:, 0:1], scr_f[:, 0:D], AX.X,
                                    ALU.add)
                    c_ = 1.0 / D
                    v.tensor_scalar(st["tmp"][:, 0:1], st["sum"][:, 0:1], c_,
                                    None, ALU.mult)
                    v.drain()
                    v.tensor_scalar(st["tmp2"][:, 0:1], st["sq"][:, 0:1], c_,
                                    None, ALU.mult)
                    v.tensor_mul(st["minv"][:, 0:1], st["tmp"][:, 0:1],
                                 st["tmp"][:, 0:1])
                    v.drain()
                    mk.inc(v.tensor_sub(st["tmp2"][:, 0:1], st["tmp2"][:, 0:1],
                                        st["minv"][:, 0:1]), "v")
                mk.v(_v1b)
                vm3 = mk.n["v"]
                def _sm(s):
                    mk.wait(s, "v", vm3)
                    mk.inc(s.activation(st["tmp2"][:, 0:1], st["tmp2"][:, 0:1],
                                        AF.Sqrt, bias=cb(EPS_LN)), "s")
                mk.s(_sm)
                sm3 = mk.n["s"]
                def _vr(v):
                    mk.wait(v, "s", sm3)
                    v.reciprocal(st["tmp2"][:, 0:1], st["tmp2"][:, 0:1])
                    v.drain()
                    v.tensor_mul(st["minv"][:, 0:1], st["tmp"][:, 0:1],
                                 st["tmp2"][:, 0:1])
                    v.drain()
                    v.tensor_scalar(x_tm[:, s_ * D:(s_ + 1) * D],
                                    x_tm[:, s_ * D:(s_ + 1) * D],
                                    st["minv"][:, 0:1], None, ALU.subtract)
                    v.drain()
                    mk.inc(v.scalar_tensor_tensor(
                        x_tm[:, s_ * D:(s_ + 1) * D], stripe_a[:],
                        st["tmp2"][:, 0:1], x_tm[:, s_ * D:(s_ + 1) * D],
                        ALU.mult, ALU.add), "v")
                mk.v(_vr)

        mk.rendezvous()

        if stop_after == "L0_attn":
            _finish(mk, out, x_tm)
            return nc

        vm, sm = lnstats(lambda s_: x_tm[:, s_ * D:(s_ + 1) * D])
        v_lnf = lnminv(st["m0"], st["i0"], wait_v=vm, wait_s=sm)
        for g2 in range(4):
            for si in range(2):
                s_ = g2 * 2 + si
                def _va(v):
                    if g2 == 0 and si == 0:
                        mk.wait(v, "v", v_lnf)
                    mk.wait(v, "s", pt_guard[0])
                    mk.inc(v.tensor_scalar(stripe_c[:],
                                           x_tm[:, s_ * D:(s_ + 1) * D],
                                           st["m0"][:, s_:s_ + 1],
                                           st["i0"][:, s_:s_ + 1],
                                           ALU.subtract, ALU.mult), "v")
                mk.v(_va)
                va = mk.n["v"]
                transpose_stripe(lambda c: stripe_c[:, c * 128:(c + 1) * 128],
                                 si, xT, wait_t=(("v", va),))
            xnT_done = mk.n["s"]
            pfg = [mk.n["s"], mk.n["s"]]
            for hc in range(128):
                wm = load_kslab(whgu, hc * 128, 128, hc % 2, dram_bf=True)
                def _t(t):
                    mk.wait(t, wm[0], wm[1])
                    if hc == 0:
                        mk.wait(t, "s", xnT_done)
                        mk.wait(t, "v", mk.n["v"])
                    mk.wait(t, "s", pfg[hc % 2])
                    b = hc % 2
                    for k in range(FC):
                        ins = t.matmul(
                            PF[:, b * 512:b * 512 + 256],
                            wslab[:, b * D + k * 128:b * D + k * 128 + 128],
                            xT[:, k * TL:k * TL + 256],
                            start=(k == 0), stop=(k == FC - 1))
                    mk.inc(ins, "t")
                mk.t(_t)
                tm = mk.n["t"]
                slab_guard[hc % 2] = tm
                def _s(s):
                    mk.wait(s, "t", tm)
                    mk.wait(s, "v", mk.n["v"])
                    b = hc % 2
                    if hc < 64:
                        ins = s.activation(big_b[:, hc * 256:(hc + 1) * 256],
                                           PF[:, b * 512:b * 512 + 256],
                                           AF.Silu)
                    else:
                        ins = s.copy(big_a[:, (hc - 64) * 256:(hc - 63) * 256],
                                     PF[:, b * 512:b * 512 + 256])
                    mk.inc(ins, "s")
                mk.s(_s)
                pfg[hc % 2] = mk.n["s"]
            gu_done = mk.n["s"]
            def _vh(v):
                mk.wait(v, "s", gu_done)
                for q in range(4):
                    ins = v.tensor_mul(big_a[:, q * 4096:(q + 1) * 4096],
                                       big_b[:, q * 4096:(q + 1) * 4096],
                                       big_a[:, q * 4096:(q + 1) * 4096])
                mk.inc(v.drain(), "v")
            mk.v(_vh)
            h_done = mk.n["v"]
            tsec = [mk.n["t"]]
            for sec in range(4):
                def _sx(s):
                    if sec == 0:
                        mk.wait(s, "v", h_done)
                    mk.wait(s, "t", tsec[0])
                    mk.inc(s.activation(gla_t[:, 0:4096],
                                        big_a[:, sec * 4096:(sec + 1) * 4096],
                                        AF.Square), "s")
                mk.s(_sx)
                sxm = mk.n["s"]
                def _tx(t):
                    mk.wait(t, "s", sxm)
                    for i in range(16):
                        ins = t.matmul(PX[0:1, 0:256], ones_bf[:],
                                       gla_t[:, i * 256:(i + 1) * 256],
                                       start=(sec == 0 and i == 0),
                                       stop=(sec == 3 and i == 15))
                    mk.inc(ins, "t")
                mk.t(_tx)
                tsec[0] = mk.n["t"]
            def _se(s):
                mk.wait(s, "t", tsec[0])
                mk.inc(s.copy(sqrow[:, 0:256], PX[0:1, 0:256]), "s")
            mk.s(_se)
            sqm = mk.n["s"]
            for si in range(2):
                def _ti(t):
                    mk.wait(t, "s", sqm)
                    mk.wait(t, "v", mk.n["v"])
                    mk.inc(t.matmul(PX[0:128, 256 + si:257 + si],
                                    sqrow[0:1, si * 128:(si + 1) * 128],
                                    ones1[0:1, 0:1],
                                    start=True, stop=True), "t")
                mk.t(_ti)
                tim = mk.n["t"]
                def _vi(v):
                    mk.wait(v, "t", tim)
                    mk.inc(v.tensor_scalar(st["tmp"][:, 4 + si:5 + si],
                                           PX[0:128, 256 + si:257 + si],
                                           1.0 / H1, EPS_RMS,
                                           ALU.mult, ALU.add), "v")
                mk.v(_vi)
                vim = mk.n["v"]
                def _si(s):
                    mk.wait(s, "v", vim)
                    s.activation(st["tmp"][:, 4 + si:5 + si],
                                 st["tmp"][:, 4 + si:5 + si], AF.Ln)
                    s.drain()
                    mk.inc(s.activation(st["tmp"][:, 4 + si:5 + si],
                                        st["tmp"][:, 4 + si:5 + si],
                                        AF.Exp, scale=-0.5), "s")
                mk.s(_si)
            sim = mk.n["s"]
            def _vi2(v):
                mk.wait(v, "s", sim)
                mk.inc(v.drain(), "v")
            mk.v(_vi2)
            v_inv = mk.n["v"]
            for k2 in range(64):
                wm = load_rowslab(whp, k2 * 128, D, k2 % 2, dram_bf=True)
                def _t(t):
                    mk.wait(t, wm[0], wm[1])
                    if k2 == 0:
                        mk.wait(t, "v", v_inv)
                        mk.wait(t, "s", mk.n["s"])
                    for si in range(2):
                        for ofg in range(4):
                            dst = (PBIG[:, ofg * 512:(ofg + 1) * 512]
                                   if si == 0 else sec1[ofg])
                            ins = t.matmul(
                                dst,
                                big_a[:, k2 * 256 + si * 128:
                                      k2 * 256 + si * 128 + 128],
                                wslab[:, (k2 % 2) * D + ofg * 512:
                                      (k2 % 2) * D + ofg * 512 + 512],
                                start=(k2 == 0), stop=(k2 == 63))
                    mk.inc(ins, "t")
                mk.t(_t)
                slab_guard[k2 % 2] = mk.n["t"]
            tm3 = mk.n["t"]
            for si in range(2):
                s_ = g2 * 2 + si
                def _ve(v):
                    if si == 0:
                        mk.wait(v, "t", tm3)
                        mk.inc(v.tensor_scalar(stripe_a[:], PBIG[:, 0:D],
                                               st["tmp"][:, 4:5], None,
                                               ALU.mult), "v")
                    else:
                        v.drain()
                        for ofg in range(4):
                            ins = v.tensor_scalar(
                                stripe_a[:, ofg * 512:(ofg + 1) * 512],
                                sec1[ofg], st["tmp"][:, 5:6], None,
                                ALU.mult)
                        mk.inc(ins, "v")
                mk.v(_ve)
                ln_post_stripe(s_, mk.n["v"])

        mk.rendezvous()
        if stop_after == "L0_ffn":
            _finish(mk, out, x_tm)
            return nc

        KT0, KT1, AOF, SGOF, OGT = 8192, 9216, 10240, 10368, 12416
        vm, sm = lnstats(lambda s_: x_tm[:, s_ * D:(s_ + 1) * D])
        v_ln1a = lnminv(st["m0"], st["i0"], wait_v=vm, wait_s=sm)
        for s_ in range(NS):
            def _va(v):
                if s_ == 0:
                    mk.wait(v, "v", v_ln1a)
                mk.wait(v, "s", pt_guard[0])
                mk.inc(v.tensor_scalar(stripe_c[:],
                                       x_tm[:, s_ * D:(s_ + 1) * D],
                                       st["m0"][:, s_:s_ + 1],
                                       st["i0"][:, s_:s_ + 1],
                                       ALU.subtract, ALU.mult), "v")
            mk.v(_va)
            va = mk.n["v"]
            transpose_stripe(lambda c: stripe_c[:, c * 128:(c + 1) * 128], s_,
                             xT, wait_t=(("v", va),))
        xnT_done = mk.n["s"]

        def _gw(g):
            mk.inc(g.dma_start(out=wgk2_sb[:],
                               in_=wgk2.rearrange("a (r n) -> a r n", n=128)),
                   "gm", 16)
        mk.g(_gw)
        wmz = load_kslab(wgk1, 0, 16, 0)
        for half in range(2):
            def _tz(t, half=half):
                if half == 0:
                    mk.wait(t, wmz[0], wmz[1])
                    mk.wait(t, "s", xnT_done)
                else:
                    mk.wait(t, "s", mk.n["s"])
                for k in range(FC):
                    ins = t.matmul(
                        PX[0:16, 0:512],
                        wslab[:, k * 16:k * 16 + 16],
                        xT[:, k * TL + half * 512:k * TL + half * 512 + 512],
                        start=(k == 0), stop=(k == FC - 1))
                mk.inc(ins, "t")
            mk.t(_tz)
            tzm = mk.n["t"]
            if half == 1:
                slab_guard[0] = tzm
            def _sz(s, half=half, tzm=tzm):
                mk.wait(s, "t", tzm)
                mk.inc(s.copy(z1_sb[0:16, half * 512:half * 512 + 512],
                              PX[0:16, 0:512]), "s")
            mk.s(_sz)
        z1_done = mk.n["s"]

        for which, w_ap, base in (("q", wq1, 0), ("k", wk1, 8192)):
            for qc in range(8):
                wm = load_kslab(w_ap, qc * 128, 128, qc % 2)
                def _t(t):
                    mk.wait(t, wm[0], wm[1])
                    mk.wait(t, "s", mk.n["s"])
                    for half in range(2):
                        for k in range(FC):
                            ins = t.matmul(
                                PF[:, half * 512:(half + 1) * 512],
                                wslab[:, (qc % 2) * D + k * 128:
                                      (qc % 2) * D + k * 128 + 128],
                                xT[:, k * TL + half * 512:
                                   k * TL + half * 512 + 512],
                                start=(k == 0), stop=(k == FC - 1))
                    mk.inc(ins, "t")
                mk.t(_t)
                tqm = mk.n["t"]
                slab_guard[qc % 2] = tqm
                def _s(s):
                    mk.wait(s, "t", tqm)
                    ins = s.copy(
                        big_a[:, base:base + 8192]
                        .rearrange("p (c n) -> p c n", n=1024)
                        [:, :, qc * 128:(qc + 1) * 128],
                        PF[:, 0:1024].rearrange("p (c n) -> p c n", n=128))
                    mk.inc(ins, "s")
                mk.s(_s)
        qk_done = mk.n["s"]

        def _vz(v):
            mk.wait(v, "s", qk_done)
            v.memset(st["Sp"][:], 0.0)
            v.memset(S_run[:], 0.0)
            mk.inc(v.drain(), "v")
        mk.v(_vz)

        def gla_gk_G(c, extra_wait_v=0):
            def _vrp(v):
                mk.wait(v, "v", extra_wait_v)
                v.drain()
                v.tensor_copy(st["Mp"][:], st["Sp"][:])
                mk.inc(v.drain(), "v")
            mk.v(_vrp)
            vrp = mk.n["v"]
            def _tz(t):
                mk.wait(t, "s", z1_done)
                mk.wait_dma(t, "gm")
                mk.wait(t, "v", mk.n["v"])
                mk.wait(t, "s", mk.n["s"])
                for r in range(8):
                    ins = t.matmul(PF[:, r * 128:(r + 1) * 128],
                                   wgk2_sb[0:16, r * 128:(r + 1) * 128],
                                   z1_sb[0:16, c * 128:(c + 1) * 128],
                                   start=True, stop=True)
                mk.inc(ins, "t")
            mk.t(_tz)
            tzm = mk.n["t"]
            def _sa(s):
                mk.wait(s, "t", tzm)
                mk.wait(s, "v", mk.n["v"])
                for r in range(8):
                    ins = s.activation(
                        scr_f[:, 1024 + r * 128:1024 + (r + 1) * 128],
                        PF[:, r * 128:(r + 1) * 128], AF.Abs,
                        bias=vecs[:, 6 * FC + r:6 * FC + r + 1])
                s.drain()
                for r in range(8):
                    ins = s.activation(
                        scr_f[:, 1024 + r * 128:1024 + (r + 1) * 128],
                        scr_f[:, 1024 + r * 128:1024 + (r + 1) * 128],
                        AF.Exp, scale=-1.0)
                s.drain()
                for r in range(8):
                    ins = s.activation(
                        scr_f[:, 1024 + r * 128:1024 + (r + 1) * 128],
                        scr_f[:, 1024 + r * 128:1024 + (r + 1) * 128],
                        AF.Ln, bias=cb(1.0))
                mk.inc(ins, "s")
            mk.s(_sa)
            sam = mk.n["s"]
            def _vg(v):
                mk.wait(v, "s", sam)
                mk.wait(v, "v", vrp)
                for r in range(8):
                    v.tensor_scalar(scr_f[:, r * 128:(r + 1) * 128],
                                    PF[:, r * 128:(r + 1) * 128],
                                    vecs[:, 6 * FC + r:6 * FC + r + 1], 0.0,
                                    ALU.add, ALU.min)
                v.drain()
                for r in range(8):
                    v.tensor_sub(scr_f[:, 1024 + r * 128:1024 + (r + 1) * 128],
                                 scr_f[:, r * 128:(r + 1) * 128],
                                 scr_f[:, 1024 + r * 128:1024 + (r + 1) * 128])
                v.drain()
                for r in range(8):
                    v.tensor_tensor_scan(
                        scr_f[:, r * 128:(r + 1) * 128],
                        ones1[:, 0:1].to_broadcast([128, 128]),
                        scr_f[:, 1024 + r * 128:1024 + (r + 1) * 128],
                        st["Mp"][:, r:r + 1], ALU.mult, ALU.add)
                v.drain()
                mk.inc(v.tensor_copy(
                    st["Sp"][:, 0:8],
                    scr_f[:, 0:1024].rearrange("p (r n) -> p r n", n=128)
                    [:, :, 127]), "v")
            mk.v(_vg)
            return mk.n["v"]

        for c in range(NS):
            vg = gla_gk_G(c)
            def _vcs(v):
                v.drain()
                mk.inc(v.tensor_copy(big_b[:, c * 1024:(c + 1) * 1024],
                                     S_run[:]), "v")
            mk.v(_vcs)
            def _vsc(v):
                v.tensor_sub(st["m1"][:], st["Sp"][:], st["Mp"][:])
                v.drain()
                mk.inc(v.tensor_scalar(st["d10b"][:], st["Sp"][:], 1.0 / 16,
                                       None, ALU.mult), "v")
            mk.v(_vsc)
            vsc = mk.n["v"]
            def _ssc(s):
                mk.wait(s, "v", vsc)
                s.activation(st["m1"][:], st["m1"][:], AF.Exp, scale=1.0 / 16)
                s.drain()
                for r in range(8):
                    ins = s.activation(
                        scr_f[:, 1024 + r * 128:1024 + (r + 1) * 128],
                        scr_f[:, r * 128:(r + 1) * 128], AF.Exp,
                        scale=-1.0 / 16,
                        bias=st["d10b"][:, r:r + 1])
                mk.inc(ins, "s")
            mk.s(_ssc)
            ssc = mk.n["s"]
            def _vk(v):
                mk.wait(v, "s", ssc)
                mk.wait(v, "s", mk.n["s"])
                mk.inc(v.tensor_mul(big_b[:, KT0:KT0 + 1024],
                                    big_a[:, 8192 + c * 1024:
                                          8192 + (c + 1) * 1024],
                                    scr_f[:, 1024:2048]), "v")
            mk.v(_vk)
            vk = mk.n["v"]
            for r0 in (0, 8):
                def _tt(t):
                    mk.wait(t, "v", vk)
                    mk.wait(t, "s", pt_guard[0])
                    for j in range(min(8, 8 - r0) if False else 8):
                        r = r0 + j
                        if r >= 8:
                            break
                        ins = t.transpose(PT[:, j * 128:(j + 1) * 128],
                                          big_b[:, KT0 + r * 128:
                                                KT0 + (r + 1) * 128],
                                          id_sb[:])
                    mk.inc(ins, "t")
                mk.t(_tt)
                ttm = mk.n["t"]
                def _st(s):
                    mk.wait(s, "t", ttm)
                    for j in range(8):
                        r = r0 + j
                        if r >= 8:
                            break
                        ins = s.copy(big_b[:, KT1 + r * 128:KT1 + (r + 1) * 128],
                                     PT[:, j * 128:(j + 1) * 128])
                    mk.inc(ins, "s")
                mk.s(_st)
                pt_guard[0] = mk.n["s"]
                if r0 == 0:
                    break
            khatT_done = mk.n["s"]
            for k in range(FC):
                wm = load_rowslab(wv1_bf, k * 128, D, k % 2, dram_bf=True)
                def _t(t):
                    mk.wait(t, wm[0], wm[1])
                    if k == 0:
                        mk.wait(t, "s", mk.n["s"])
                    for ofg in range(4):
                        ins = t.matmul(
                            PBIG[:, ofg * 512:(ofg + 1) * 512],
                            xT[:, k * TL + c * 128:k * TL + c * 128 + 128],
                            wslab[:, (k % 2) * D + ofg * 512:
                                  (k % 2) * D + ofg * 512 + 512],
                            start=(k == 0), stop=(k == FC - 1))
                    mk.inc(ins, "t")
                mk.t(_t)
                slab_guard[k % 2] = mk.n["t"]
            tvm = mk.n["t"]
            def _sv(s):
                mk.wait(s, "t", tvm)
                mk.wait(s, "v", mk.n["v"])
                mk.inc(s.copy(stripe_c[:], PBIG[:, 0:D]), "s")
            mk.s(_sv)
            sv = mk.n["s"]
            for r in range(8):
                def _td(t):
                    mk.wait(t, "s", khatT_done)
                    mk.wait(t, "s", sv)
                    mk.wait(t, "v", mk.n["v"])
                    for hh in range(2):
                        off = hh * 64
                        h = 2 * r + hh
                        ins = t.matmul(
                            PX[off:off + 64, 0:128],
                            big_b[:, KT1 + r * 128 + off:
                                  KT1 + r * 128 + off + 64],
                            stripe_c[:, h * 128:(h + 1) * 128],
                            start=True, stop=True,
                            tile_position=(0, off))
                    mk.inc(ins, "t")
                mk.t(_td)
                tdm = mk.n["t"]
                def _vd(v):
                    mk.wait(v, "t", tdm)
                    v.drain()
                    mk.inc(v.scalar_tensor_tensor(
                        S_run[:, r * 128:(r + 1) * 128],
                        S_run[:, r * 128:(r + 1) * 128],
                        st["m1"][:, r:r + 1],
                        PX[0:128, 0:128],
                        ALU.mult, ALU.add), "v")
                mk.v(_vd)
        v_sw0 = mk.n["v"]

        def _yp(y):
            mk.wait(y, "v", v_sw0)
            mk.inc(y.dma_start(out=b_gla_i[:, 0:1024], in_=S_run[:]), "y", 16)
            mk.inc(y.dma_start(out=b_gla_i[:, 1024:1032], in_=st["Sp"][:]),
                   "y", 16)
        mk.y(_yp)
        def _gg(g):
            mk.wait_dma(g, "y")
            if SKIP_CC:
                for j in range(NCORES):
                    mk.inc(g.dma_start(out=b_gla_o[j * 128:(j + 1) * 128, :],
                                       in_=b_gla_i[:, :]), "gm", 16)
                g.wait_ge(mk.h["gm"], mk.n["gm"])
                mk.inc(g.drain(), "c")
                return
            mk.inc(g.collective_compute(
                "AllGather", ALU.bypass, replica_groups=[list(range(NCORES))],
                ins=[b_gla_i.opt()], outs=[b_gla_o.opt()]), "c")
        mk.g(_gg)
        c_gla = mk.n["c"]
        def _vp0(v):
            mk.wait(v, "c", c_gla)
            v.memset(S_run[:], 0.0)
            v.memset(scr_f[:, 1024:2048], 0.0)
            mk.inc(v.drain(), "v")
        mk.v(_vp0)
        for j in range(7):
            def _gj(g):
                mk.wait(g, "c", c_gla)
                mk.wait(g, "v", mk.n["v"])
                mk.inc(g.dma_start(out=scr_f[:, 0:1024],
                                   in_=b_gla_o[j * 128:(j + 1) * 128, 0:1024]),
                       "gm", 16)
                mk.inc(g.dma_start(out=st["tmp2"][:, 0:8],
                                   in_=b_gla_o[j * 128:(j + 1) * 128,
                                               1024:1032]), "gm", 16)
            mk.g(_gj)
            def _sj(s):
                mk.wait_dma(s, "gm")
                mk.wait(s, "v", mk.n["v"])
                mk.inc(s.activation(st["tmp"][:, 0:8], st["tmp2"][:, 0:8],
                                    AF.Exp, scale=1.0 / 16), "s")
            mk.s(_sj)
            sjm = mk.n["s"]
            def _vj(v):
                mk.wait(v, "s", sjm)
                v.drain()
                for r in range(8):
                    v.scalar_tensor_tensor(
                        S_run[:, r * 128:(r + 1) * 128],
                        S_run[:, r * 128:(r + 1) * 128],
                        st["tmp"][:, r:r + 1],
                        scr_f[:, r * 128:(r + 1) * 128],
                        ALU.mult, ALU.add)
                v.drain()
                mk.inc(v.scalar_tensor_tensor(
                    scr_f[:, 1024:2048], S_run[:],
                    cmask_sb[:, j:j + 1], scr_f[:, 1024:2048],
                    ALU.mult, ALU.add), "v")
            mk.v(_vj)
        def _vpf(v):
            v.drain()
            v.tensor_copy(S_run[:], scr_f[:, 1024:2048])
            v.memset(st["Sp"][:], 0.0)
            mk.inc(v.drain(), "v")
        mk.v(_vpf)
        v_carry = mk.n["v"]

        for c in range(NS):
            vg = gla_gk_G(c, extra_wait_v=v_carry if c == 0 else 0)
            def _sl(s):
                mk.wait(s, "v", vg)
                mk.inc(s.activation(st["i1"][:], st["Mp"][:], AF.Exp,
                                    scale=1.0 / 16), "s")
            mk.s(_sl)
            slm = mk.n["s"]
            def _vsg(v):
                mk.wait(v, "s", slm)
                mk.wait(v, "t", mk.n["t"])
                for r in range(8):
                    ins = v.scalar_tensor_tensor(
                        big_b[:, KT1 + r * 128:KT1 + (r + 1) * 128],
                        S_run[:, r * 128:(r + 1) * 128],
                        st["i1"][:, r:r + 1],
                        big_b[:, c * 1024 + r * 128:c * 1024 + (r + 1) * 128],
                        ALU.mult, ALU.add)
                mk.inc(ins, "v")
            mk.v(_vsg)
            def _vbq(v):
                mk.inc(v.tensor_scalar(st["d10a"][:], st["Mp"][:], -1.0 / 16,
                                       -LN8, ALU.mult, ALU.add), "v")
            mk.v(_vbq)
            vbq = mk.n["v"]
            def _sq_(s):
                mk.wait(s, "v", vbq)
                for r in range(8):
                    ins = s.activation(
                        scr_f[:, 1024 + r * 128:1024 + (r + 1) * 128],
                        scr_f[:, r * 128:(r + 1) * 128], AF.Exp,
                        scale=1.0 / 16, bias=st["d10a"][:, r:r + 1])
                mk.inc(ins, "s")
            mk.s(_sq_)
            sqf = mk.n["s"]
            def _vqt(v):
                mk.wait(v, "s", sqf)
                mk.wait(v, "t", mk.n["t"])
                mk.inc(v.tensor_mul(gla_t[:, 2048:3072],
                                    big_a[:, c * 1024:(c + 1) * 1024],
                                    scr_f[:, 1024:2048]), "v")
            mk.v(_vqt)
            vqt = mk.n["v"]
            def _vbk(v):
                mk.inc(v.tensor_scalar(st["d10b"][:], st["Mp"][:], 1.0 / 16,
                                       None, ALU.mult), "v")
            mk.v(_vbk)
            vbk = mk.n["v"]
            def _sk(s):
                mk.wait(s, "v", vqt)
                mk.wait(s, "v", vbk)
                for r in range(8):
                    ins = s.activation(
                        scr_f[:, 1024 + r * 128:1024 + (r + 1) * 128],
                        scr_f[:, r * 128:(r + 1) * 128], AF.Exp,
                        scale=-1.0 / 16, bias=st["d10b"][:, r:r + 1])
                mk.inc(ins, "s")
            mk.s(_sk)
            skf = mk.n["s"]
            def _vkt(v):
                mk.wait(v, "s", skf)
                mk.wait(v, "t", mk.n["t"])
                mk.inc(v.tensor_mul(big_b[:, KT0:KT0 + 1024],
                                    big_a[:, 8192 + c * 1024:
                                          8192 + (c + 1) * 1024],
                                    scr_f[:, 1024:2048]), "v")
            mk.v(_vkt)
            vkt = mk.n["v"]
            for k in range(FC):
                wm = load_rowslab(wv1_bf, k * 128, D, k % 2, dram_bf=True)
                def _t(t):
                    mk.wait(t, wm[0], wm[1])
                    if k == 0:
                        mk.wait(t, "s", mk.n["s"])
                    for ofg in range(4):
                        ins = t.matmul(
                            PBIG[:, ofg * 512:(ofg + 1) * 512],
                            xT[:, k * TL + c * 128:k * TL + c * 128 + 128],
                            wslab[:, (k % 2) * D + ofg * 512:
                                  (k % 2) * D + ofg * 512 + 512],
                            start=(k == 0), stop=(k == FC - 1))
                    mk.inc(ins, "t")
                mk.t(_t)
                slab_guard[k % 2] = mk.n["t"]
            tvm = mk.n["t"]
            def _sv(s):
                mk.wait(s, "t", tvm)
                mk.wait(s, "v", mk.n["v"])
                mk.inc(s.copy(stripe_c[:], PBIG[:, 0:D]), "s")
            mk.s(_sv)
            sv = mk.n["s"]
            vsg_m = mk.n["v"]
            px_free = [mk.n["v"]]
            for h in range(16):
                r, off = h // 2, (h % 2) * 64
                def _ta(t):
                    mk.wait(t, "v", vkt)
                    mk.wait(t, "v", vqt)
                    mk.wait(t, "v", px_free[0])
                    mk.inc(t.matmul(
                        PX[0:128, 0:128],
                        big_b[off:off + 64, KT0 + r * 128:KT0 + (r + 1) * 128],
                        gla_t[off:off + 64, 2048 + r * 128:
                              2048 + (r + 1) * 128],
                        start=True, stop=True, tile_position=(off, 0)), "t")
                mk.t(_ta)
                tam = mk.n["t"]
                def _vam(v):
                    mk.wait(v, "t", tam)
                    mk.wait(v, "t", mk.n["t"])
                    mk.inc(v.tensor_mul(big_b[:, AOF:AOF + 128],
                                        PX[0:128, 0:128], masktri_sb[:]),
                           "v")
                mk.v(_vam)
                px_free[0] = mk.n["v"]
                vam = mk.n["v"]
                def _to(t):
                    mk.wait(t, "v", vam)
                    mk.wait(t, "s", sv)
                    mk.wait(t, "v", vsg_m)
                    t.matmul(PBIG[:, h * 128:(h + 1) * 128],
                             big_b[:, AOF:AOF + 128],
                             stripe_c[:, h * 128:(h + 1) * 128],
                             start=True, stop=False)
                    mk.inc(t.matmul(
                        PBIG[:, h * 128:(h + 1) * 128],
                        gla_t[off:off + 64, 2048 + r * 128:
                              2048 + (r + 1) * 128],
                        big_b[off:off + 64, KT1 + r * 128:KT1 + (r + 1) * 128],
                        start=False, stop=True, tile_position=(off, 0)), "t")
                mk.t(_to)
            to_done = mk.n["t"]
            def _vo(v):
                mk.wait(v, "t", to_done)
                v.tensor_copy(stripe_a[:], PBIG[:, 0:D])
                mk.inc(v.drain(), "v")
            mk.v(_vo)
            voc = mk.n["v"]
            def _so2(s):
                mk.wait(s, "v", voc)
                mk.inc(s.activation(scr_f[:, 0:D], stripe_a[:], AF.Square),
                       "s")
            mk.s(_so2)
            so2 = mk.n["s"]
            def _vo1(v):
                mk.wait(v, "s", so2)
                v.tensor_reduce(eloc[:], scr_f[:, 0:D]
                                .rearrange("p (h n) -> p h n", n=128),
                                AX.X, ALU.add)
                v.drain()
                mk.inc(v.tensor_scalar(ebuf[:], eloc[:], 1.0 / 128, EPS_LN,
                                       ALU.mult, ALU.add), "v")
            mk.v(_vo1)
            vom = mk.n["v"]
            if stop_after == "DBGGLA" and c == 0:
                def _yd(y):
                    mk.wait(y, "v", vom)
                    mk.inc(y.dma_start(out=out[0:128, :], in_=stripe_a[:]),
                           "y", 16)
                    mk.inc(y.dma_start(out=out[128:256, 0:16], in_=eloc[:]),
                           "y", 16)
                    mk.inc(y.dma_start(out=out[128:256, 16:32], in_=ebuf[:]),
                           "y", 16)
                    mk.inc(y.dma_start(out=out[256:384, 0:1024],
                                       in_=scr_f[:, 0:1024]), "y", 16)
                    mk.wait_dma(y, "y")
                mk.y(_yd)
                def _gd(g):
                    mk.wait(g, "v", vom)
                    mk.inc(g.dma_start(out=out[384:512, 0:1024],
                                       in_=gla_t[:, 2048:3072]), "gm", 16)
                    mk.inc(g.dma_start(out=out[512:640, 0:1024],
                                       in_=big_b[:, KT0:KT0 + 1024]), "gm", 16)
                    mk.inc(g.dma_start(out=out[640:768, 0:1024],
                                       in_=big_b[:, KT1:KT1 + 1024]), "gm", 16)
                    mk.inc(g.dma_start(out=out[768:896, 0:2048],
                                       in_=stripe_c[:]), "gm", 16)
                    mk.inc(g.dma_start(out=out[896:1024, 0:1024],
                                       in_=big_b[:, 0:1024]), "gm", 16)
                    mk.wait_dma(g, "gm")
                mk.g(_gd)
                mk.rendezvous()
                return nc
            def _so(s):
                mk.wait(s, "v", vom)
                s.activation(ebuf[:], ebuf[:], AF.Ln)
                s.drain()
                mk.inc(s.activation(ebuf[:], ebuf[:], AF.Exp, scale=-0.5),
                       "s")
            mk.s(_so)
            som = mk.n["s"]
            def _vo2(v):
                mk.wait(v, "s", som)
                v.tensor_mul(stripe_a[:]
                             .rearrange("p (h n) -> p h n", n=128),
                             stripe_a[:]
                             .rearrange("p (h n) -> p h n", n=128),
                             ebuf[:].unsqueeze(2).to_broadcast([128, 16, 128]))
                v.drain()
                mk.inc(v.tensor_mul(stripe_a[:], stripe_a[:], normw_b[:]),
                       "v")
            mk.v(_vo2)
            von = mk.n["v"]
            for k in range(FC):
                wm = load_rowslab(wg1_bf, k * 128, D, k % 2, dram_bf=True)
                def _t(t):
                    mk.wait(t, wm[0], wm[1])
                    if k == 0:
                        mk.wait(t, "v", von)
                    for ofg in range(4):
                        ins = t.matmul(
                            PBIG[:, ofg * 512:(ofg + 1) * 512],
                            xT[:, k * TL + c * 128:k * TL + c * 128 + 128],
                            wslab[:, (k % 2) * D + ofg * 512:
                                  (k % 2) * D + ofg * 512 + 512],
                            start=(k == 0), stop=(k == FC - 1))
                    mk.inc(ins, "t")
                mk.t(_t)
                slab_guard[k % 2] = mk.n["t"]
            tgm = mk.n["t"]
            def _sg(s):
                mk.wait(s, "t", tgm)
                mk.wait(s, "t", mk.n["t"])
                mk.inc(s.activation(big_b[:, SGOF:SGOF + D], PBIG[:, 0:D],
                                    AF.Silu), "s")
            mk.s(_sg)
            sgm = mk.n["s"]
            def _vog(v):
                mk.wait(v, "s", sgm)
                mk.inc(v.tensor_mul(big_b[:, SGOF:SGOF + D], stripe_a[:],
                                    big_b[:, SGOF:SGOF + D]), "v")
            mk.v(_vog)
            vog = mk.n["v"]
            for c0 in (0, 8):
                def _tt(t):
                    mk.wait(t, "v", vog)
                    mk.wait(t, "s", pt_guard[0])
                    for j in range(8):
                        cc2 = c0 + j
                        ins = t.transpose(PT[:, j * 128:(j + 1) * 128],
                                          big_b[:, SGOF + cc2 * 128:
                                                SGOF + (cc2 + 1) * 128],
                                          id_sb[:])
                    mk.inc(ins, "t")
                mk.t(_tt)
                ttm = mk.n["t"]
                def _st(s):
                    mk.wait(s, "t", ttm)
                    for j in range(8):
                        cc2 = c0 + j
                        ins = s.copy(big_b[:, OGT + cc2 * 128:
                                           OGT + (cc2 + 1) * 128],
                                     PT[:, j * 128:(j + 1) * 128])
                    mk.inc(ins, "s")
                mk.s(_st)
                pt_guard[0] = mk.n["s"]
            ogt_done = mk.n["s"]
            for k in range(FC):
                wm = load_rowslab(wo1_bf, k * 128, D, k % 2, dram_bf=True)
                def _t(t):
                    mk.wait(t, wm[0], wm[1])
                    if k == 0:
                        mk.wait(t, "s", ogt_done)
                        mk.wait(t, "v", mk.n["v"])
                    for ofg in range(4):
                        ins = t.matmul(
                            PBIG[:, ofg * 512:(ofg + 1) * 512],
                            big_b[:, OGT + k * 128:OGT + (k + 1) * 128],
                            wslab[:, (k % 2) * D + ofg * 512:
                                  (k % 2) * D + ofg * 512 + 512],
                            start=(k == 0), stop=(k == FC - 1))
                    mk.inc(ins, "t")
                mk.t(_t)
                slab_guard[k % 2] = mk.n["t"]
            twm = mk.n["t"]
            def _vbo(v):
                mk.wait(v, "t", twm)
                mk.inc(v.tensor_copy(stripe_a[:], PBIG[:, 0:D]), "v")
            mk.v(_vbo)
            ln_post_stripe(c, mk.n["v"])

        mk.rendezvous()
        if stop_after == "L1_attn":
            _finish(mk, out, x_tm)
            return nc

        def _gb1(g):
            mk.inc(g.dma_start(
                out=bias1_b, in_=b1.unsqueeze(0).to_broadcast([128, D])),
                "gm", 16)
        mk.g(_gb1)
        vm, sm = lnstats(lambda s_: x_tm[:, s_ * D:(s_ + 1) * D])
        v_lng = lnminv(st["m0"], st["i0"], wait_v=vm, wait_s=sm)
        for g2 in range(4):
            for si in range(2):
                s_ = g2 * 2 + si
                def _va(v):
                    if g2 == 0 and si == 0:
                        mk.wait(v, "v", v_lng)
                    mk.wait(v, "s", pt_guard[0])
                    mk.inc(v.tensor_scalar(stripe_c[:],
                                           x_tm[:, s_ * D:(s_ + 1) * D],
                                           st["m0"][:, s_:s_ + 1],
                                           st["i0"][:, s_:s_ + 1],
                                           ALU.subtract, ALU.mult), "v")
                mk.v(_va)
                va = mk.n["v"]
                transpose_stripe(lambda c: stripe_c[:, c * 128:(c + 1) * 128],
                                 si, xT, wait_t=(("v", va),))
            xnT_done = mk.n["s"]
            pfg = [mk.n["s"], mk.n["s"]]
            for oc in range(64):
                wm = load_kslab(w0t_bf, oc * 128, 128, oc % 2, dram_bf=True)
                def _t(t):
                    mk.wait(t, wm[0], wm[1])
                    if oc == 0:
                        mk.wait(t, "s", xnT_done)
                        mk.wait(t, "v", mk.n["v"])
                    mk.wait(t, "s", pfg[oc % 2])
                    b = oc % 2
                    for k in range(FC):
                        ins = t.matmul(
                            PF[:, b * 512:b * 512 + 256],
                            wslab[:, b * D + k * 128:b * D + k * 128 + 128],
                            xT[:, k * TL:k * TL + 256],
                            start=(k == 0), stop=(k == FC - 1))
                    mk.inc(ins, "t")
                mk.t(_t)
                tm = mk.n["t"]
                slab_guard[oc % 2] = tm
                def _s(s):
                    mk.wait(s, "t", tm)
                    mk.wait(s, "t", mk.n["t"])
                    b = oc % 2
                    mk.inc(s.activation(big_a[:, oc * 256:(oc + 1) * 256],
                                        PF[:, b * 512:b * 512 + 256],
                                        AF.Gelu,
                                        bias=b0_sb[:, oc:oc + 1]), "s")
                mk.s(_s)
                pfg[oc % 2] = mk.n["s"]
            d_done = mk.n["s"]
            for k2 in range(64):
                wm = load_rowslab(w1t_bf, k2 * 128, D, k2 % 2, dram_bf=True)
                def _t(t):
                    mk.wait(t, wm[0], wm[1])
                    if k2 == 0:
                        mk.wait(t, "s", d_done)
                        mk.wait(t, "v", mk.n["v"])
                    for si in range(2):
                        for ofg in range(4):
                            dst = (PBIG[:, ofg * 512:(ofg + 1) * 512]
                                   if si == 0 else sec1[ofg])
                            ins = t.matmul(
                                dst,
                                big_a[:, k2 * 256 + si * 128:
                                      k2 * 256 + si * 128 + 128],
                                wslab[:, (k2 % 2) * D + ofg * 512:
                                      (k2 % 2) * D + ofg * 512 + 512],
                                start=(k2 == 0), stop=(k2 == 63))
                    mk.inc(ins, "t")
                mk.t(_t)
                slab_guard[k2 % 2] = mk.n["t"]
            tm3 = mk.n["t"]
            for si in range(2):
                s_ = g2 * 2 + si
                def _ve(v):
                    if si == 0:
                        mk.wait(v, "t", tm3)
                        mk.wait_dma(v, "gm")
                        mk.inc(v.tensor_add(stripe_a[:], PBIG[:, 0:D],
                                            bias1_b), "v")
                    else:
                        v.drain()
                        for ofg in range(4):
                            ins = v.tensor_add(
                                stripe_a[:, ofg * 512:(ofg + 1) * 512],
                                sec1[ofg],
                                bias1_b[:, ofg * 512:(ofg + 1) * 512])
                        mk.inc(ins, "v")
                mk.v(_ve)
                ln_post_stripe(s_, mk.n["v"])

        mk.rendezvous()
        _finish(mk, out, x_tm)
    return nc


def _finish(mk, out, x_tm):
    NS_, D_ = NS, D
    def _y(y):
        for s_ in range(NS_):
            mk.inc(y.dma_start(out=out[s_ * 128:(s_ + 1) * 128, :],
                               in_=x_tm[:, s_ * D_:(s_ + 1) * D_]), "y", 16)
        mk.wait_dma(y, "y")
    mk.y(_y)


_NC_CACHE = {}


def _consts():
    ident = np.eye(128, dtype=ml_dtypes.bfloat16)
    onesblk = np.zeros((128, 2), dtype=ml_dtypes.bfloat16)
    onesblk[0:64, 0] = 1
    onesblk[64:128, 1] = 1
    masktri = np.triu(np.ones((128, 128), dtype=np.float32)) \
        .astype(ml_dtypes.bfloat16)
    return ident, onesblk, masktri


def kernel(L0_attn_x, L0_attn_time_decay, L0_attn_time_first, L0_attn_Wk,
           L0_attn_Wv, L0_attn_Wr, L0_attn_Wo, L0_attn_Wfk, L0_attn_Wfv,
           L0_attn_ln1_w, L0_attn_ln1_b, L0_attn_ln2_w, L0_attn_ln2_b,
           L0_ffn_w_gate_up, L0_ffn_w_proj, L1_attn_Wq, L1_attn_Wk,
           L1_attn_Wv, L1_attn_Wg, L1_attn_Wgk1, L1_attn_Wgk2, L1_attn_bgk2,
           L1_attn_norm_w, L1_attn_Wo, L1_ffn_W0, L1_ffn_bias0, L1_ffn_W1,
           L1_ffn_bias1):
    key = STOP_AFTER
    if key not in _NC_CACHE:
        _NC_CACHE[key] = build_program(stop_after=STOP_AFTER)
    nc = _NC_CACHE[key]
    asf = lambda a: np.ascontiguousarray(np.asarray(a), dtype=np.float32)
    x = asf(L0_attn_x).reshape(8192, D)
    wgu_t = np.ascontiguousarray(asf(L0_ffn_w_gate_up).T)
    wp_t = np.ascontiguousarray(asf(L0_ffn_w_proj).T)
    w0_t = np.ascontiguousarray(asf(L1_ffn_W0).T)
    w1_t = np.ascontiguousarray(asf(L1_ffn_W1).T)
    ident, onesblk, masktri = _consts()
    base = dict(
        L0Wk=asf(L0_attn_Wk), L0Wv=asf(L0_attn_Wv), L0Wr=asf(L0_attn_Wr),
        L0Wo=asf(L0_attn_Wo), L0Wfk=asf(L0_attn_Wfk), L0Wfv=asf(L0_attn_Wfv),
        L0td=asf(L0_attn_time_decay).reshape(-1),
        L0tf=asf(L0_attn_time_first).reshape(-1),
        L0ln1w=asf(L0_attn_ln1_w), L0ln1b=asf(L0_attn_ln1_b),
        L0ln2w=asf(L0_attn_ln2_w), L0ln2b=asf(L0_attn_ln2_b),
        L1Wq=asf(L1_attn_Wq), L1Wk=asf(L1_attn_Wk), L1Wv=asf(L1_attn_Wv),
        L1Wg=asf(L1_attn_Wg), L1Wgk1=asf(L1_attn_Wgk1),
        L1Wgk2=asf(L1_attn_Wgk2), L1bgk2=asf(L1_attn_bgk2),
        L1normw=asf(L1_attn_norm_w), L1Wo=asf(L1_attn_Wo),
        W0T=w0_t, b0=asf(L1_ffn_bias0), W1T=w1_t, b1=asf(L1_ffn_bias1),
        ident=ident, onesblk=onesblk, masktri=masktri,
    )
    in_maps = []
    for i in range(NCORES):
        cm = np.zeros((128, 8), dtype=np.float32)
        if i >= 1:
            cm[:, i - 1] = 1.0
        m0 = np.full((128, 1), 1.0 if i == 0 else 0.0, dtype=np.float32)
        im = dict(base)
        im["x_sh"] = np.ascontiguousarray(x[i * TL:(i + 1) * TL])
        im["wgu_sl"] = np.ascontiguousarray(wgu_t[i * 256:(i + 1) * 256])
        im["wp_sl"] = np.ascontiguousarray(wp_t[i * 1024:(i + 1) * 1024])
        im["cmask"] = cm
        im["mask0"] = m0
        in_maps.append(im)
    res = run_bass_kernel_spmd(nc, in_maps, core_ids=list(range(NCORES)))
    full = np.concatenate([res.results[i]["out"] for i in range(NCORES)],
                          axis=0)
    return full.reshape(1, 8192, D)


# version 08
# speedup vs baseline: 1.0473x; 1.0473x over previous
import os as _os
import numpy as np
import ml_dtypes
import concourse.bass as bass
import concourse.mybir as mybir
from concourse.bass_utils import run_bass_kernel_spmd
from contextlib import ExitStack

F32 = mybir.dt.float32
BF16 = mybir.dt.bfloat16
AF = mybir.ActivationFunctionType
ALU = mybir.AluOpType
AX = mybir.AxisListType

NCORES = 8
TL = 1024
D = 2048
NS = 8
FC = 16
H1 = 4 * D
LN8 = float(np.log(8.0))
EPS_LN = 1e-5
EPS_RMS = 1.1920929e-7
THR_C = 0.5 * (1.0 + 2.0 ** -9)

STOP_AFTER = None


class MK:
    def __init__(self, nc, block, es):
        self.nc, self.block, self.es = nc, block, es
        self.n, self.h = {}, {}
        for name in ("t", "v", "s", "c", "y", "g0", "g1", "gm", "gb", "yb", "c0", "c1", "c2", "c3", "d0", "d1", "w0"):
            self.h[name] = es.enter_context(nc.semaphore("sem_" + name))
            self.n[name] = 0

    def t(self, f): self.block.tensor(f)
    def v(self, f): self.block.vector(f)
    def s(self, f): self.block.scalar(f)
    def g(self, f): self.block.gpsimd(f)
    def y(self, f): self.block.sync(f)

    def inc(self, instr, sem, k=1):
        instr.then_inc(self.h[sem], k)
        self.n[sem] += k
        return self.n[sem]

    def wait(self, eng, sem, val):
        if val > 0:
            eng.wait_ge(self.h[sem], val)

    def wait_dma(self, eng, sem):
        self.wait(eng, sem, self.n[sem])

    def rendezvous(self):
        bar = {"t": "t", "v": "v", "s": "s", "g": "gb", "y": "yb"}
        def _fence_y(y):
            for s_ in ("y", "c0", "c1", "c2", "c3", "w0"):
                self.wait_dma(y, s_)
        self.y(_fence_y)
        def _fence_g(g):
            for s_ in ("g0", "g1", "gm", "d0", "d1"):
                self.wait_dma(g, s_)
        self.g(_fence_g)
        marks = {}
        for name, emit in (("t", self.t), ("v", self.v), ("s", self.s),
                           ("g", self.g), ("y", self.y)):
            def _inc(e, name=name):
                self.inc(e.drain(), bar[name])
            emit(_inc)
            marks[bar[name]] = self.n[bar[name]]
        for name, emit in (("t", self.t), ("v", self.v), ("s", self.s),
                           ("g", self.g), ("y", self.y)):
            def _w(e, name=name):
                for o, mv in marks.items():
                    if o != bar[name]:
                        e.wait_ge(self.h[o], mv)
            emit(_w)


def build_program(stop_after=None):
    nc = bass.Bass("TRN2", target_bir_lowering=False)

    def din(name, shape, dt=F32):
        return nc.dram_tensor(name, shape, dt, kind="ExternalInput").ap()

    x_in = din("x_sh", [TL, D])
    out = nc.dram_tensor("out", [TL, D], F32, kind="ExternalOutput").ap()

    wk0 = din("L0Wk", [D, D]); wv0 = din("L0Wv", [D, D])
    wr0 = din("L0Wr", [D, D]); wo0 = din("L0Wo", [D, D])
    wfk = din("L0Wfk", [D, H1]); wfv = din("L0Wfv", [H1, D])
    tdec = din("L0td", [D]); tfirst = din("L0tf", [D])
    ln1w = din("L0ln1w", [D]); ln1b = din("L0ln1b", [D])
    ln2w = din("L0ln2w", [D]); ln2b = din("L0ln2b", [D])
    wgu_sl = din("wgu_sl", [D // 8, 2 * H1])
    wp_sl = din("wp_sl", [H1 // 8, D])
    wq1 = din("L1Wq", [D, D // 2]); wk1 = din("L1Wk", [D, D // 2])
    wv1 = din("L1Wv", [D, D]); wg1 = din("L1Wg", [D, D])
    wgk1 = din("L1Wgk1", [D, 16]); wgk2 = din("L1Wgk2", [16, D // 2])
    bgk2 = din("L1bgk2", [D // 2]); normw = din("L1normw", [128])
    wo1 = din("L1Wo", [D, D])
    w0t = din("W0T", [D, H1]); b0 = din("b0", [H1])
    w1t = din("W1T", [H1, D]); b1 = din("b1", [D])
    ident = din("ident", [128, 128], BF16)
    onesblk = din("onesblk", [128, 2], BF16)
    masktri = din("masktri", [128, 128], BF16)
    cmask = din("cmask", [128, 8])
    mask0 = din("mask0", [128, 1])

    _loc = _os.environ.get("SIM_LOCAL") == "1"

    def dint(name, shape, dt=F32, shared=False):
        return nc.dram_tensor(name, shape, dt,
                              addr_space="Shared" if (shared and not _loc)
                              else "Local").ap()

    b_rwkv_i = dint("b_rwkv_i", [128, FC])
    b_rwkv_o = dint("b_rwkv_o", [128 * NCORES, FC], shared=True)
    b_gla_i = dint("b_gla_i", [128, 1032])
    b_gla_o = dint("b_gla_o", [128 * NCORES, 1032], shared=True)
    b_wgu_i = dint("b_wgu_i", [D // 8, 2 * H1], BF16)
    whgu = dint("whgu", [D, 2 * H1], BF16, shared=True)
    b_wp_i = dint("b_wp_i", [H1 // 8, D], BF16)
    whp = dint("whp", [H1, D], BF16, shared=True)
    s_gu_d = dint("s_gu_d", [4, 2 * H1])
    s_wp_d = dint("s_wp_d", [16, D])
    wfv_bf = dint("wfv_bf", [H1, D], BF16)
    w1t_bf = dint("w1t_bf", [H1, D], BF16)
    wv1_bf = dint("wv1_bf", [D, D], BF16)
    wg1_bf = dint("wg1_bf", [D, D], BF16)
    wo1_bf = dint("wo1_bf", [D, D], BF16)
    wo0_bf = dint("wo0_bf", [D, D], BF16)

    es = ExitStack()
    with es:
        sb = lambda n_, s_, dt=F32: es.enter_context(nc.sbuf_tensor(n_, s_, dt))
        x_tm = sb("x_tm", [128, NS * D])
        xT = sb("xT", [128, FC * TL], BF16)
        big_a = sb("big_a", [128, FC * TL], BF16)
        big_b = sb("big_b", [128, FC * TL], BF16)
        wslab = sb("wslab", [128, 2 * FC * 128], BF16)
        stripe_a = sb("stripe_a", [128, D])
        stripe_c = sb("stripe_c", [128, D], BF16)
        scr_f = sb("scr_f", [128, 2 * TL])
        id_sb = sb("id_sb", [128, 128], BF16)
        masktri_sb = sb("masktri_sb", [128, 128], BF16)
        cmask_sb = sb("cmask_sb", [128, 8])
        mask0_sb = sb("mask0_sb", [128, 1])
        ones1 = sb("ones1", [128, 1])
        onesblk_sb = sb("onesblk_sb", [128, 2], BF16)
        normw_b = sb("normw_b", [128, D], BF16)
        b0_sb = sb("b0_sb", [128, 64])
        vecs = sb("vecs", [128, FC * 8])
        st = {}
        for nm in ("sum", "sq", "m0", "i0", "sd0", "m1", "i1", "tmp", "tmp2",
                   "Mp", "Sp", "minv", "d10a", "d10b"):
            st[nm] = sb("st_" + nm, [128, 8])
        eloc = sb("eloc", [128, FC])
        Prw = sb("Prw", [128, FC])
        ebuf = sb("ebuf", [128, FC])
        v0c = sb("v0c", [128, FC])
        bonus = sb("bonus", [128, FC])
        gla_t = sb("gla_t", [128, 4096], BF16)
        S_run = sb("S_run", [128, 1024])
        ones_bf = sb("ones_bf", [128, 1], BF16)
        _CVALS = [EPS_LN, EPS_RMS, 1.0, -LN8]
        cons = sb("cons", [128, len(_CVALS)])

        def cb(val):
            i = _CVALS.index(val)
            return cons[:, i:i + 1]
        PBIG = es.enter_context(nc.psum_tensor("PBIG", [128, 2048], F32))
        PF = es.enter_context(nc.psum_tensor("PF", [128, 1024], F32))
        PT = es.enter_context(nc.psum_tensor("PT", [128, 1024], BF16))
        PX = es.enter_context(nc.psum_tensor("PX", [128, 512], F32))
        PTF = PT.bitcast(F32)

        bias1_b = gla_t[:, 0:D]
        sqrow = stripe_c[:].bitcast(F32)[0:1, 0:256]
        z1_sb = gla_t[:, 1024:1536].bitcast(F32).bitcast(BF16)[0:16, 0:1024] \
            if False else gla_t[0:16, 1024:2048]
        wgk2_sb = gla_t[0:16, 3072:4096]
        sst_f = scr_f[0:2, 512:1024]
        sst_b = scr_f[0:2, 1024:1536].bitcast(BF16)

        block = es.enter_context(nc.Block())
        mk = MK(nc, block, es)

        def ph0(y):
            for s_ in range(NS):
                mk.inc(y.dma_start(out=x_tm[:, s_ * D:(s_ + 1) * D],
                                   in_=x_in[s_ * 128:(s_ + 1) * 128, :]), "y", 16)
            for dst, src in ((id_sb, ident), (masktri_sb, masktri),
                             (cmask_sb, cmask), (mask0_sb, mask0),
                             (onesblk_sb, onesblk)):
                mk.inc(y.dma_start(out=dst[:], in_=src), "y", 16)
            for slot, vec in ((0, tdec), (1, tfirst), (2, ln1w), (3, ln1b),
                              (4, ln2w), (5, ln2b)):
                mk.inc(y.dma_start(
                    out=vecs[:, slot * FC:(slot + 1) * FC],
                    in_=vec.rearrange("(f p) -> p f", p=128),
                    allow_slow_non_contiguous=True), "y", 16)
            mk.inc(y.dma_start(
                out=vecs[:, 6 * FC:6 * FC + 8],
                in_=bgk2.rearrange("(f p) -> p f", p=128),
                allow_slow_non_contiguous=True), "y", 16)
            mk.inc(y.dma_start(
                out=b0_sb[:], in_=b0.rearrange("(f p) -> p f", p=128),
                allow_slow_non_contiguous=True), "y", 16)
        mk.y(ph0)
        y0 = mk.n["y"]

        def ph0g(g):
            mk.inc(g.dma_start(
                out=normw_b[:],
                in_=normw.unsqueeze(0).unsqueeze(0)
                .to_broadcast([128, FC, 128])), "gm", 16)
        mk.g(ph0g)

        def ph0v(v):
            mk.wait_dma(v, "y")
            for i, val in enumerate(_CVALS):
                v.memset(cons[:, i:i + 1], val)
            v.memset(ones_bf[:], 1.0)
            mk.inc(v.memset(ones1[:], 1.0), "v")
        mk.v(ph0v)

        def ph0s(s):
            mk.wait_dma(s, "y")
            s.activation(vecs[:, 0:FC], vecs[:, 0:FC], AF.Exp)
            s.drain()
            s.activation(vecs[:, 0:FC], vecs[:, 0:FC], AF.Exp, scale=-1.0)
            s.drain()
            mk.inc(s.activation(vecs[:, FC:2 * FC], vecs[:, FC:2 * FC],
                                AF.Exp), "s")
        mk.s(ph0s)
        conv_list = [(wfv, wfv_bf, H1), (w1t, w1t_bf, H1), (wv1, wv1_bf, D),
                     (wg1, wg1_bf, D), (wo1, wo1_bf, D), (wo0, wo0_bf, D)]
        stripe_bf = stripe_a.bitcast(BF16)
        stg = [wslab[:, 0:D], wslab[:, D:2 * D],
               stripe_bf[:, 0:D], stripe_bf[:, D:2 * D]]
        ld_sem = ["g0", "g1", "d0", "d1"]
        st_sem = ["c0", "c1", "c2", "c3"]
        ci = 0
        for w_ap, w_bf, rows in conv_list:
            for r0 in range(0, rows, 128):
                b = ci % 4
                ci += 1
                def _g(g):
                    mk.wait_dma(g, st_sem[b])
                    mk.inc(g.dma_start(
                        out=stg[b],
                        in_=w_ap[r0:r0 + 128, 0:D]), ld_sem[b], 16)
                mk.g(_g)
                def _y(y):
                    mk.wait_dma(y, ld_sem[b])
                    mk.inc(y.dma_start(
                        out=w_bf[r0:r0 + 128, 0:D],
                        in_=stg[b]), st_sem[b], 16)
                mk.y(_y)
        mk.rendezvous()

        def lnstats(x_fn, n_str=NS, wait_v=0, wait_s=0):
            dumps = (stripe_a, scr_f)
            vred = [0, 0]
            for s_ in range(n_str):
                def _s(s):
                    if s_ == 0:
                        mk.wait(s, "v", wait_v)
                    mk.wait(s, "v", vred[s_ % 2])
                    mk.inc(s.activation(dumps[s_ % 2][:, 0:D], x_fn(s_),
                                        AF.Square), "s")
                mk.s(_s)
                sm_ = mk.n["s"]
                def _v(v):
                    if s_ == 0:
                        mk.wait(v, "s", wait_s)
                    v.tensor_reduce(st["sum"][:, s_:s_ + 1], x_fn(s_),
                                    AX.X, ALU.add)
                    mk.wait(v, "s", sm_)
                    mk.inc(v.tensor_reduce(st["sq"][:, s_:s_ + 1],
                                           dumps[s_ % 2][:, 0:D], AX.X,
                                           ALU.add), "v")
                mk.v(_v)
                vred[s_ % 2] = mk.n["v"]
            return mk.n["v"], mk.n["s"]

        def lnminv(m_t, i_t, sd_t=None, nfeat=D, eps=EPS_LN, n_str=NS,
                   wait_v=0, wait_s=0):
            c = 1.0 / nfeat
            def _v(v):
                mk.wait(v, "v", wait_v)
                mk.wait(v, "s", wait_s)
                v.drain()
                v.tensor_scalar(m_t[:, 0:n_str], st["sum"][:, 0:n_str], c,
                                None, ALU.mult)
                v.tensor_scalar(st["tmp"][:, 0:n_str], st["sq"][:, 0:n_str], c,
                                None, ALU.mult)
                v.drain()
                v.tensor_mul(st["tmp2"][:, 0:n_str], m_t[:, 0:n_str],
                             m_t[:, 0:n_str])
                v.drain()
                mk.inc(v.tensor_sub(st["tmp"][:, 0:n_str],
                                    st["tmp"][:, 0:n_str],
                                    st["tmp2"][:, 0:n_str]), "v")
            mk.v(_v)
            vm = mk.n["v"]
            def _s(s):
                mk.wait(s, "v", vm)
                s.activation(st["tmp2"][:, 0:n_str], st["tmp"][:, 0:n_str],
                             AF.Ln, bias=cb(eps))
                s.drain()
                s.activation(i_t[:, 0:n_str], st["tmp2"][:, 0:n_str],
                             AF.Exp, scale=-0.5)
                if sd_t is not None:
                    s.activation(sd_t[:, 0:n_str], st["tmp2"][:, 0:n_str],
                                 AF.Exp, scale=0.5)
                mk.inc(s.drain(), "s")
            mk.s(_s)
            sm = mk.n["s"]
            def _v2(v):
                mk.wait(v, "s", sm)
                mk.inc(v.drain(), "v")
            mk.v(_v2)
            return mk.n["v"]

        pt_guard = [0]

        def transpose_stripe(src_fn, s_idx, dst_sb, gw=None, gb=None,
                             wait_t=()):
            for c0 in (0, 8):
                def _t(t):
                    for sem, val in wait_t:
                        mk.wait(t, sem, val)
                    mk.wait(t, "s", pt_guard[0])
                    for j in range(8):
                        ins = t.transpose(PT[:, j * 128:(j + 1) * 128],
                                          src_fn(c0 + j), id_sb[:])
                    mk.inc(ins, "t")
                mk.t(_t)
                tm = mk.n["t"]
                def _s(s):
                    mk.wait(s, "t", tm)
                    for j in range(8):
                        c = c0 + j
                        dcol = c * TL + s_idx * 128
                        if gw is not None:
                            ins = s.activation(dst_sb[:, dcol:dcol + 128],
                                               PT[:, j * 128:(j + 1) * 128],
                                               AF.Identity, scale=gw(c), bias=gb(c))
                        else:
                            ins = s.copy(dst_sb[:, dcol:dcol + 128],
                                         PT[:, j * 128:(j + 1) * 128])
                    mk.inc(ins, "s")
                mk.s(_s)
                pt_guard[0] = mk.n["s"]

        slab_guard = [0, 0]

        def load_kslab(w_ap, of0, ncols, buf, krows=D):
            sem = "g0" if buf == 0 else "g1"
            def _g(g):
                mk.wait(g, "t", slab_guard[buf])
                ins = g.dma_start(
                    out=wslab[:, buf * D:buf * D + (krows // 128) * ncols],
                    in_=w_ap[0:krows, of0:of0 + ncols]
                    .rearrange("(k p) n -> p k n", p=128))
                mk.inc(ins, sem, 16)
            mk.g(_g)
            return (sem, mk.n[sem])

        def load_rowslab(w_ap, row0, ncols, buf, dram_bf=False):
            if dram_bf:
                sem = "w0" if buf == 0 else "g1"
                def _do(e):
                    mk.wait(e, "t", slab_guard[buf])
                    ins = e.dma_start(
                        out=wslab[:, buf * D:buf * D + ncols],
                        in_=w_ap[row0:row0 + 128, 0:ncols])
                    mk.inc(ins, sem, 16)
                if buf == 0:
                    mk.y(_do)
                else:
                    mk.g(_do)
                return (sem, mk.n[sem])
            sem = "g0" if buf == 0 else "g1"
            def _do(e):
                mk.wait(e, "t", slab_guard[buf])
                ins = e.dma_start(
                    out=wslab[:, buf * D:buf * D + ncols],
                    in_=w_ap[row0:row0 + 128, 0:ncols])
                mk.inc(ins, sem, 16)
            mk.g(_do)
            return (sem, mk.n[sem])

        def ln_post_stripe(s_, vbo):
            def _sq(s):
                mk.wait(s, "v", vbo)
                mk.inc(s.activation(scr_f[:, 0:D], stripe_a[:], AF.Square),
                       "s")
            mk.s(_sq)
            sqm = mk.n["s"]
            def _v1(v):
                mk.wait(v, "s", sqm)
                v.tensor_reduce(st["sum"][:, 0:1], stripe_a[:], AX.X, ALU.add)
                v.tensor_reduce(st["sq"][:, 0:1], scr_f[:, 0:D], AX.X, ALU.add)
                c_ = 1.0 / D
                v.drain()
                v.tensor_scalar(st["tmp"][:, 0:1], st["sum"][:, 0:1], c_,
                                None, ALU.mult)
                v.tensor_scalar(st["tmp2"][:, 0:1], st["sq"][:, 0:1], c_,
                                None, ALU.mult)
                v.drain()
                v.tensor_mul(st["minv"][:, 0:1], st["tmp"][:, 0:1],
                             st["tmp"][:, 0:1])
                v.drain()
                mk.inc(v.tensor_sub(st["tmp2"][:, 0:1], st["tmp2"][:, 0:1],
                                    st["minv"][:, 0:1]), "v")
            mk.v(_v1)
            vm_ = mk.n["v"]
            def _sm(s):
                mk.wait(s, "v", vm_)
                s.activation(st["tmp2"][:, 0:1], st["tmp2"][:, 0:1],
                             AF.Ln, bias=cb(EPS_LN))
                s.drain()
                mk.inc(s.activation(st["tmp2"][:, 0:1], st["tmp2"][:, 0:1],
                                    AF.Exp, scale=-0.5), "s")
            mk.s(_sm)
            sm_ = mk.n["s"]
            def _vr(v):
                mk.wait(v, "s", sm_)
                v.tensor_mul(st["minv"][:, 0:1], st["tmp"][:, 0:1],
                             st["tmp2"][:, 0:1])
                v.drain()
                v.tensor_scalar(x_tm[:, s_ * D:(s_ + 1) * D],
                                x_tm[:, s_ * D:(s_ + 1) * D],
                                st["minv"][:, 0:1], None, ALU.subtract)
                v.drain()
                mk.inc(v.scalar_tensor_tensor(
                    x_tm[:, s_ * D:(s_ + 1) * D], stripe_a[:],
                    st["tmp2"][:, 0:1], x_tm[:, s_ * D:(s_ + 1) * D],
                    ALU.mult, ALU.add), "v")
            mk.v(_vr)
            return mk.n["v"]

        qsl = stripe_c
        thrB = scr_f

        def quant_slice(w_sl, nrc, ncc, csz, s_dram, bounce):
            for r in range(nrc):
                for cc in range(ncc):
                    def _g(g):
                        mk.wait(g, "s", mk.n["s"])
                        ins = g.dma_start(
                            out=qsl[:, 0:csz],
                            in_=w_sl[r * 128:(r + 1) * 128,
                                     cc * csz:(cc + 1) * csz])
                        mk.inc(ins, "gm", 16)
                    mk.g(_g)
                    def _s(s):
                        mk.wait_dma(s, "gm")
                        mk.wait(s, "t", mk.n["t"])
                        mk.wait(s, "v", mk.n["v"])
                        mk.inc(s.activation(qsl[:, csz:2 * csz], qsl[:, 0:csz],
                                            AF.Abs), "s")
                    mk.s(_s)
                    sm = mk.n["s"]
                    def _t(t):
                        mk.wait(t, "s", sm)
                        mk.wait(t, "v", mk.n["v"])
                        mk.inc(t.matmul(PX[0:2, 0:csz], onesblk_sb[:],
                                        qsl[:, csz:2 * csz],
                                        start=True, stop=True), "t")
                    mk.t(_t)
                    tm = mk.n["t"]
                    def _v(v):
                        mk.wait(v, "t", tm)
                        mk.wait_dma(v, "y")
                        v.tensor_scalar(sst_b[:, 0:csz], PX[0:2, 0:csz],
                                        1.0 / 64, 1e-8, ALU.mult, ALU.max)
                        v.drain()
                        mk.inc(v.tensor_scalar(sst_f[:, 0:csz], sst_b[:, 0:csz],
                                               THR_C, None, ALU.mult), "v")
                    mk.v(_v)
                    vm = mk.n["v"]
                    def _y(y):
                        mk.wait(y, "v", vm)
                        mk.inc(y.dma_start(
                            out=s_dram[2 * r:2 * r + 2, cc * csz:(cc + 1) * csz],
                            in_=sst_f[:, 0:csz]), "y", 16)
                    mk.y(_y)
            for r in range(nrc):
                for cc in range(ncc):
                    def _y2(y):
                        mk.wait(y, "v", mk.n["v"])
                        ins = y.dma_start(
                            out=thrB[:, 0:csz],
                            in_=s_dram[2 * r:2 * r + 2, cc * csz:(cc + 1) * csz]
                            .unsqueeze(1).to_broadcast([2, 64, csz]))
                        mk.inc(ins, "y", 16)
                    mk.y(_y2)
                    def _g2(g):
                        mk.wait(g, "s", mk.n["s"])
                        ins = g.dma_start(
                            out=qsl[:, 0:csz],
                            in_=w_sl[r * 128:(r + 1) * 128,
                                     cc * csz:(cc + 1) * csz])
                        mk.inc(ins, "gm", 16)
                    mk.g(_g2)
                    def _s2(s):
                        mk.wait_dma(s, "gm")
                        mk.wait(s, "v", mk.n["v"])
                        s.activation(qsl[:, csz:2 * csz], qsl[:, 0:csz], AF.Abs)
                        mk.inc(s.sign(qsl[:, 2 * csz:3 * csz], qsl[:, 0:csz]),
                               "s")
                    mk.s(_s2)
                    sm2 = mk.n["s"]
                    def _v2(v):
                        mk.wait(v, "s", sm2)
                        mk.wait_dma(v, "y")
                        v.tensor_tensor(qsl[:, csz:2 * csz],
                                        qsl[:, csz:2 * csz],
                                        thrB[:, 0:csz], ALU.is_ge)
                        v.drain()
                        v.tensor_mul(qsl[:, csz:2 * csz], qsl[:, csz:2 * csz],
                                     qsl[:, 2 * csz:3 * csz])
                        v.drain()
                        mk.inc(v.scalar_tensor_tensor(
                            qsl[:, 3 * csz:4 * csz], thrB[:, 0:csz],
                            1.0 / THR_C, qsl[:, csz:2 * csz],
                            ALU.mult, ALU.mult), "v")
                    mk.v(_v2)
                    vm2 = mk.n["v"]
                    def _y3(y):
                        mk.wait(y, "v", vm2)
                        mk.inc(y.dma_start(
                            out=bounce[r * 128:(r + 1) * 128,
                                       cc * csz:(cc + 1) * csz],
                            in_=qsl[:, 3 * csz:4 * csz]), "y", 16)
                    mk.y(_y3)

        quant_slice(wgu_sl, 2, 32, 512, s_gu_d, b_wgu_i)
        quant_slice(wp_sl, 8, 4, 512, s_wp_d, b_wp_i)
        yq = mk.n["y"]

        SKIP_CC = _os.environ.get("SKIP_CC") == "1"
        SKIP_CC_BIG = SKIP_CC or _os.environ.get("SKIP_CC_BIG") == "1"
        def _gq(g):
            mk.wait_dma(g, "y")
            if SKIP_CC_BIG:
                for j in range(NCORES):
                    mk.inc(g.dma_start(
                        out=whgu[j * (D // 8):(j + 1) * (D // 8), :],
                        in_=b_wgu_i[:, :]), "gm", 16)
                    mk.inc(g.dma_start(
                        out=whp[j * (H1 // 8):(j + 1) * (H1 // 8), :],
                        in_=b_wp_i[:, :]), "gm", 16)
                g.wait_ge(mk.h["gm"], mk.n["gm"])
                mk.inc(g.drain(), "c")
                mk.inc(g.drain(), "c")
                return
            mk.inc(g.collective_compute(
                "AllGather", ALU.bypass, replica_groups=[list(range(NCORES))],
                ins=[b_wgu_i.opt()], outs=[whgu.opt()]), "c")
            mk.inc(g.collective_compute(
                "AllGather", ALU.bypass, replica_groups=[list(range(NCORES))],
                ins=[b_wp_i.opt()], outs=[whp.opt()]), "c")
        mk.g(_gq)
        c_whq = mk.n["c"]
        mk.rendezvous()

        vm, sm = lnstats(lambda s_: x_tm[:, s_ * D:(s_ + 1) * D])
        v_ln0 = lnminv(st["m0"], st["i0"], sd_t=st["sd0"], wait_v=vm, wait_s=sm)
        if stop_after == "DBG00":
            def _yd(y):
                mk.wait(y, "v", v_ln0)
                for i, nm in enumerate(("m0", "i0", "sum", "sq", "sd0",
                                        "tmp", "tmp2")):
                    mk.inc(y.dma_start(out=out[0:128, i * 8:(i + 1) * 8],
                                       in_=st[nm][:, 0:8]), "y", 16)
                mk.inc(y.dma_start(out=out[0:128, 56:60],
                                   in_=cons[:, 0:4]), "y", 16)
                mk.wait(y, "y", mk.n["y"])
            mk.y(_yd)
            mk.rendezvous()
            return nc
        sq_s = [0]
        for s_ in range(NS):
            def _vx(v):
                mk.wait(v, "s", sq_s[0])
                v.tensor_scalar(stripe_a[:], x_tm[:, s_ * D:(s_ + 1) * D],
                                st["m0"][:, s_:s_ + 1], st["i0"][:, s_:s_ + 1],
                                ALU.subtract, ALU.mult)
                v.drain()
                mk.inc(v.tensor_reduce(st["sum"][:, s_:s_ + 1], stripe_a[:],
                                       AX.X, ALU.add), "v")
            mk.v(_vx)
            vxm = mk.n["v"]
            def _sx(s):
                mk.wait(s, "v", vxm)
                mk.inc(s.activation(scr_f[:, 0:D], stripe_a[:], AF.Square), "s")
            mk.s(_sx)
            sq_s[0] = mk.n["s"]
            sxm = mk.n["s"]
            def _vx2(v):
                mk.wait(v, "s", sxm)
                mk.inc(v.tensor_reduce(st["sq"][:, s_:s_ + 1], scr_f[:, 0:D],
                                       AX.X, ALU.add), "v")
            mk.v(_vx2)
        vm = mk.n["v"]
        lnminv(st["m1"], st["i1"], wait_v=vm)
        def _vc(v):
            v.drain()
            v.tensor_mul(st["Mp"][:], st["m1"][:], st["sd0"][:])
            v.drain()
            v.tensor_add(st["Mp"][:], st["Mp"][:], st["m0"][:])
            mk.inc(v.tensor_mul(st["Sp"][:], st["i0"][:], st["i1"][:]), "v")
        mk.v(_vc)
        vcomp = mk.n["v"]
        if stop_after == "DBG0":
            def _yd(y):
                mk.wait(y, "v", vcomp)
                for i, nm in enumerate(("m0", "i0", "m1", "i1", "Mp", "Sp",
                                        "sum", "sq", "sd0")):
                    mk.inc(y.dma_start(out=out[0:128, i * 8:(i + 1) * 8],
                                       in_=st[nm][:, 0:8]), "y", 16)
                mk.wait(y, "y", mk.n["y"])
            mk.y(_yd)
            mk.rendezvous()
            return nc
        if stop_after == "DBG1":
            def _vd(v):
                mk.wait(v, "v", vcomp)
                for s_ in range(NS):
                    ins = v.tensor_scalar(x_tm[:, s_ * D:(s_ + 1) * D],
                                          x_tm[:, s_ * D:(s_ + 1) * D],
                                          st["Mp"][:, s_:s_ + 1],
                                          st["Sp"][:, s_:s_ + 1],
                                          ALU.subtract, ALU.mult)
                mk.inc(ins, "v")
            mk.v(_vd)
            mk.rendezvous()
            _finish(mk, out, x_tm)
            return nc
        for s_ in range(NS):
            def _va(v):
                if s_ == 0:
                    mk.wait(v, "v", vcomp)
                    v.drain()
                mk.wait(v, "s", pt_guard[0])
                mk.inc(v.tensor_scalar(stripe_c[:],
                                       x_tm[:, s_ * D:(s_ + 1) * D],
                                       st["Mp"][:, s_:s_ + 1],
                                       st["Sp"][:, s_:s_ + 1],
                                       ALU.subtract, ALU.mult), "v")
            mk.v(_va)
            va = mk.n["v"]
            transpose_stripe(lambda c: stripe_c[:, c * 128:(c + 1) * 128], s_,
                             xT,
                             gw=lambda c: vecs[:, 2 * FC + c:2 * FC + c + 1],
                             gb=lambda c: vecs[:, 3 * FC + c:3 * FC + c + 1],
                             wait_t=(("v", va),))
        xT_done = mk.n["s"]

        kv_guard = [0]
        uv_guard = [0]
        for c in range(FC):
            gk = load_kslab(wk0, c * 128, 128, 0)
            gv = load_kslab(wv0, c * 128, 128, 1)
            def _t(t):
                mk.wait(t, gk[0], gk[1])
                mk.wait(t, gv[0], gv[1])
                if c == 0:
                    mk.wait(t, "s", xT_done)
                mk.wait(t, "s", kv_guard[0])
                for half in range(2):
                    for k in range(FC):
                        ins = t.matmul(
                            PF[:, half * 512:(half + 1) * 512],
                            wslab[:, k * 128:k * 128 + 128],
                            xT[:, k * TL + half * 512:k * TL + half * 512 + 512],
                            start=(k == 0), stop=(k == FC - 1))
                mk.inc(ins, "t")
                mk.wait(t, "v", uv_guard[0])
                for half in range(2):
                    for k in range(FC):
                        ins = t.matmul(
                            PBIG[:, half * 512:(half + 1) * 512],
                            wslab[:, D + k * 128:D + k * 128 + 128],
                            xT[:, k * TL + half * 512:k * TL + half * 512 + 512],
                            start=(k == 0), stop=(k == FC - 1))
                mk.inc(ins, "t")
            mk.t(_t)
            tk, tv = mk.n["t"] - 1, mk.n["t"]
            slab_guard[0] = tk
            slab_guard[1] = tv
            def _s(s):
                mk.wait(s, "t", tk)
                mk.wait(s, "v", mk.n["v"])
                mk.inc(s.activation(scr_f[:, 0:TL], PF[:, 0:TL], AF.Exp), "s")
            mk.s(_s)
            kv_guard[0] = mk.n["s"]
            se = mk.n["s"]
            def _v(v):
                mk.wait(v, "s", se)
                mk.wait(v, "t", tv)
                mk.inc(v.tensor_mul(big_a[:, c * TL:(c + 1) * TL],
                                    scr_f[:, 0:TL], PBIG[:, 0:TL]), "v")
                v.tensor_copy(v0c[:, c:c + 1], PBIG[:, 0:1])
                v.drain()
                v.tensor_tensor_scan(
                    scr_f[:, TL:2 * TL],
                    vecs[:, c:c + 1].to_broadcast([128, TL]),
                    big_a[:, c * TL:(c + 1) * TL],
                    0.0, ALU.mult, ALU.add)
                v.drain()
                mk.inc(v.tensor_copy(eloc[:, c:c + 1],
                                     scr_f[:, 2 * TL - 1:2 * TL]), "v")
            mk.v(_v)
            uv_guard[0] = mk.n["v"]
        v_scan = mk.n["v"]
        if stop_after == "DBG2":
            def _yd(y):
                mk.wait(y, "v", v_scan)
                mk.inc(y.dma_start(
                    out=out.rearrange("(g p) d -> p g d", p=128),
                    in_=big_a[:, 0:FC * TL].rearrange("p (g d) -> p g d", d=D)),
                    "y", 16)
                mk.wait(y, "y", mk.n["y"])
            def _gd(g):
                mk.wait(g, "v", v_scan)
                mk.inc(g.dma_start(
                    out=out.rearrange("(g p) d -> p g d", p=128),
                    in_=big_a[:, 0:FC * TL].rearrange("p (g d) -> p g d", d=D)),
                    "gm", 16)
                mk.wait_dma(g, "gm")
            mk.g(_gd)
            mk.rendezvous()
            return nc

        def _ye(y):
            mk.wait(y, "v", v_scan)
            mk.inc(y.dma_start(out=b_rwkv_i[:, :], in_=eloc[:]), "y", 16)
        mk.y(_ye)
        def _gc(g):
            mk.wait_dma(g, "y")
            if SKIP_CC:
                for j in range(NCORES):
                    mk.inc(g.dma_start(out=b_rwkv_o[j * 128:(j + 1) * 128, :],
                                       in_=b_rwkv_i[:, :]), "gm", 16)
                g.wait_ge(mk.h["gm"], mk.n["gm"])
                mk.inc(g.drain(), "c")
                return
            mk.inc(g.collective_compute(
                "AllGather", ALU.bypass, replica_groups=[list(range(NCORES))],
                ins=[b_rwkv_i.opt()], outs=[b_rwkv_o.opt()]), "c")
        mk.g(_gc)
        c_rw = mk.n["c"]
        def _sd(s):
            mk.wait(s, "v", mk.n["v"])
            s.activation(st["d10a"][:], vecs[:, 0:8], AF.Ln)
            s.activation(st["d10b"][:], vecs[:, 8:FC], AF.Ln)
            s.drain()
            s.activation(st["d10a"][:], st["d10a"][:], AF.Exp, scale=float(TL))
            mk.inc(s.activation(st["d10b"][:], st["d10b"][:], AF.Exp,
                                scale=float(TL)), "s")
        mk.s(_sd)
        s_d1024 = mk.n["s"]
        def _gl(g):
            mk.wait(g, "c", c_rw)
            mk.wait(g, "v", mk.n["v"])
            mk.inc(g.dma_start(
                out=scr_f[:, 0:7 * FC].rearrange("p (j f) -> p j f", f=FC),
                in_=b_rwkv_o[0:7 * 128, :]
                .rearrange("(j p) f -> p j f", p=128)), "gm", 16)
        mk.g(_gl)
        def _vp(v):
            mk.wait_dma(v, "gm")
            mk.wait(v, "s", s_d1024)
            v.memset(ebuf[:], 0.0)
            v.memset(Prw[:], 0.0)
            for j in range(7):
                v.drain()
                v.tensor_mul(ebuf[:, 0:8], ebuf[:, 0:8], st["d10a"][:])
                v.tensor_mul(ebuf[:, 8:FC], ebuf[:, 8:FC], st["d10b"][:])
                v.drain()
                v.tensor_add(ebuf[:], ebuf[:], scr_f[:, j * FC:(j + 1) * FC])
                v.drain()
                mk.inc(v.scalar_tensor_tensor(
                    Prw[:], ebuf[:], cmask_sb[:, j:j + 1], Prw[:],
                    ALU.mult, ALU.add), "v")
            v.drain()
            mk.inc(v.tensor_mul(bonus[:], vecs[:, FC:2 * FC], v0c[:]), "v")
        mk.v(_vp)
        v_prw = mk.n["v"]
        if stop_after == "DBG3":
            def _yd(y):
                mk.wait(y, "v", v_prw)
                mk.inc(y.dma_start(out=out[0:128, 0:FC], in_=eloc[:]), "y", 16)
                mk.inc(y.dma_start(out=out[0:128, 16:32], in_=Prw[:]), "y", 16)
                mk.inc(y.dma_start(out=out[0:128, 32:48], in_=ebuf[:]), "y", 16)
                mk.inc(y.dma_start(out=out[0:128, 48:160],
                                   in_=scr_f[:, 0:112]), "y", 16)
                mk.inc(y.dma_start(out=out[0:128, 160:176],
                                   in_=v0c[:]), "y", 16)
                mk.inc(y.dma_start(out=out[0:128, 176:184],
                                   in_=st["d10a"][:]), "y", 16)
                mk.inc(y.dma_start(out=out[0:128, 184:192],
                                   in_=st["d10b"][:]), "y", 16)
                mk.wait(y, "y", mk.n["y"])
            mk.y(_yd)
            mk.rendezvous()
            return nc

        rw_guard = [kv_guard[0]]
        for c in range(FC):
            gr = load_kslab(wr0, c * 128, 128, c % 2)
            def _t(t):
                mk.wait(t, gr[0], gr[1])
                mk.wait(t, "s", rw_guard[0])
                for half in range(2):
                    for k in range(FC):
                        ins = t.matmul(
                            PF[:, half * 512:(half + 1) * 512],
                            wslab[:, (c % 2) * D + k * 128:
                                  (c % 2) * D + k * 128 + 128],
                            xT[:, k * TL + half * 512:k * TL + half * 512 + 512],
                            start=(k == 0), stop=(k == FC - 1))
                mk.inc(ins, "t")
            mk.t(_t)
            tr = mk.n["t"]
            slab_guard[c % 2] = tr
            def _s(s):
                mk.wait(s, "t", tr)
                mk.wait(s, "v", mk.n["v"])
                mk.inc(s.activation(big_b[:, c * TL:(c + 1) * TL], PF[:, 0:TL],
                                    AF.Sigmoid), "s")
            mk.s(_s)
            rw_guard[0] = mk.n["s"]
            sr = mk.n["s"]
            def _v(v):
                mk.wait(v, "s", sr)
                if c == 0:
                    mk.wait(v, "v", v_prw)
                v.drain()
                v.tensor_tensor_scan(
                    scr_f[:, 0:TL],
                    vecs[:, c:c + 1].to_broadcast([128, TL]),
                    big_a[:, c * TL:(c + 1) * TL],
                    Prw[:, c:c + 1], ALU.mult, ALU.add)
                v.drain()
                v.scalar_tensor_tensor(
                    scr_f[:, 0:1], bonus[:, c:c + 1], mask0_sb[:, 0:1],
                    scr_f[:, 0:1], ALU.mult, ALU.add)
                v.drain()
                mk.inc(v.tensor_mul(big_a[:, c * TL:(c + 1) * TL],
                                    big_b[:, c * TL:(c + 1) * TL],
                                    scr_f[:, 0:TL]), "v")
            mk.v(_v)
        v_rwkv = mk.n["v"]
        if stop_after == "DBG35":
            def _gd(g):
                mk.wait(g, "v", v_rwkv)
                mk.inc(g.dma_start(
                    out=out.rearrange("(g p) d -> p g d", p=128),
                    in_=big_a[:, 0:FC * TL].rearrange("p (g d) -> p g d", d=D)),
                    "gm", 16)
                mk.wait_dma(g, "gm")
            mk.g(_gd)
            mk.rendezvous()
            return nc

        po_guard = [0]
        for s_ in range(NS):
            for k in range(FC):
                wm = load_rowslab(wo0_bf, k * 128, D, k % 2, dram_bf=True)
                def _t(t):
                    mk.wait(t, wm[0], wm[1])
                    if k == 0:
                        mk.wait(t, "v", v_rwkv)
                        mk.wait(t, "v", po_guard[0])
                    for ofg in range(4):
                        ins = t.matmul(
                            PBIG[:, ofg * 512:(ofg + 1) * 512],
                            big_a[:, k * TL + s_ * 128:k * TL + s_ * 128 + 128],
                            wslab[:, (k % 2) * D + ofg * 512:
                                  (k % 2) * D + ofg * 512 + 512],
                            start=(k == 0), stop=(k == FC - 1))
                    mk.inc(ins, "t")
                mk.t(_t)
                slab_guard[k % 2] = mk.n["t"]
            tm = mk.n["t"]
            def _v(v):
                mk.wait(v, "t", tm)
                for ofg in range(4):
                    v.tensor_scalar(scr_f[:, ofg * 512:ofg * 512 + 512],
                                    x_tm[:, s_ * D + ofg * 512:
                                         s_ * D + ofg * 512 + 512],
                                    st["m0"][:, s_:s_ + 1],
                                    st["i0"][:, s_:s_ + 1],
                                    ALU.subtract, ALU.mult)
                v.drain()
                for ofg in range(4):
                    ins = v.tensor_add(
                        big_b[:, s_ * D + ofg * 512:s_ * D + ofg * 512 + 512],
                        scr_f[:, ofg * 512:ofg * 512 + 512],
                        PBIG[:, ofg * 512:(ofg + 1) * 512])
                mk.inc(ins, "v")
            mk.v(_v)
            po_guard[0] = mk.n["v"]
            if stop_after == "DBG37" and s_ == int(_os.environ.get("DBG_STRIPE", "0")):
                def _sd2(s):
                    mk.wait(s, "t", tm)
                    mk.wait(s, "v", po_guard[0])
                    mk.inc(s.copy(stripe_a[:], PBIG[:, 0:D]), "s")
                mk.s(_sd2)
                sdm = mk.n["s"]
                def _yd(y):
                    mk.wait(y, "s", sdm)
                    mk.inc(y.dma_start(out=out[0:128, :], in_=stripe_a[:]),
                           "y", 16)
                    mk.wait(y, "v", po_guard[0])
                    mk.inc(y.dma_start(out=out[128:256, 0:512],
                                       in_=scr_f[:, 0:512]), "y", 16)
                    mk.wait(y, "y", mk.n["y"])
                mk.y(_yd)
                def _gd(g):
                    mk.wait(g, "v", po_guard[0])
                    mk.inc(g.dma_start(out=out[256:384, :],
                                       in_=wslab[:, D:2 * D]), "gm", 16)
                    for q in range(4):
                        mk.inc(g.dma_start(
                            out=out[384 + q * 128:512 + q * 128, :],
                            in_=big_b[:, q * D:(q + 1) * D]), "gm", 16)
                    mk.wait_dma(g, "gm")
                mk.g(_gd)
                mk.rendezvous()
                return nc
        def _vdr(v):
            mk.inc(v.drain(), "v")
        mk.v(_vdr)
        v_x1 = mk.n["v"]
        if stop_after == "DBG4":
            def _gd(g):
                mk.wait(g, "v", v_x1)
                for s_ in range(NS):
                    ins = g.dma_start(out=out[s_ * 128:(s_ + 1) * 128, :],
                                      in_=big_b[:, s_ * D:(s_ + 1) * D])
                    mk.inc(ins, "gm", 16)
                mk.wait_dma(g, "gm")
            mk.g(_gd)
            mk.rendezvous()
            return nc

        vm, sm = lnstats(lambda s_: big_b[:, s_ * D:(s_ + 1) * D],
                         wait_v=v_x1, wait_s=0)
        v_ln2 = lnminv(st["m1"], st["i1"], wait_v=vm, wait_s=sm)
        hid = big_a
        for gi in range(4):
            for si in range(2):
                s_ = gi * 2 + si
                def _va(v):
                    if gi == 0 and si == 0:
                        mk.wait(v, "v", v_ln2)
                    mk.wait(v, "s", pt_guard[0])
                    mk.inc(v.tensor_scalar(stripe_c[:],
                                           big_b[:, s_ * D:(s_ + 1) * D],
                                           st["m1"][:, s_:s_ + 1],
                                           st["i1"][:, s_:s_ + 1],
                                           ALU.subtract, ALU.mult), "v")
                mk.v(_va)
                va = mk.n["v"]
                transpose_stripe(lambda c: stripe_c[:, c * 128:(c + 1) * 128],
                                 si, xT,
                                 gw=lambda c: vecs[:, 4 * FC + c:4 * FC + c + 1],
                                 gb=lambda c: vecs[:, 5 * FC + c:5 * FC + c + 1],
                                 wait_t=(("v", va),))
            xl2T_done = mk.n["s"]
            pf_guard2 = [0, 0]
            for c2 in range(64):
                wm = load_kslab(wfk, c2 * 128, 128, c2 % 2)
                def _t(t):
                    mk.wait(t, wm[0], wm[1])
                    if c2 == 0:
                        mk.wait(t, "s", xl2T_done)
                        mk.wait(t, "v", mk.n["v"])
                    mk.wait(t, "s", pf_guard2[c2 % 2])
                    b = c2 % 2
                    for k in range(FC):
                        ins = t.matmul(
                            PF[:, b * 512:b * 512 + 256],
                            wslab[:, b * D + k * 128:b * D + k * 128 + 128],
                            xT[:, k * TL:k * TL + 256],
                            start=(k == 0), stop=(k == FC - 1))
                    mk.inc(ins, "t")
                mk.t(_t)
                tm2 = mk.n["t"]
                slab_guard[c2 % 2] = tm2
                def _s(s):
                    mk.wait(s, "t", tm2)
                    b = c2 % 2
                    mk.wait(s, "v", mk.n["v"])
                    mk.inc(s.activation(stripe_a[:, b * 256:b * 256 + 256],
                                        PF[:, b * 512:b * 512 + 256],
                                        AF.Relu), "s")
                mk.s(_s)
                pf_guard2[c2 % 2] = mk.n["s"]
                srl = mk.n["s"]
                def _vsq(v):
                    mk.wait(v, "s", srl)
                    b = c2 % 2
                    mk.inc(v.tensor_mul(hid[:, c2 * 256:(c2 + 1) * 256],
                                        stripe_a[:, b * 256:b * 256 + 256],
                                        stripe_a[:, b * 256:b * 256 + 256]),
                           "v")
                mk.v(_vsq)
            hid_done = ("v", mk.n["v"])
            sec1 = [PF[:, 0:512], PF[:, 512:1024], PX[:, 0:512],
                    PTF[:, 0:512]]
            for k2 in range(64):
                wm = load_rowslab(wfv_bf, k2 * 128, D, k2 % 2, dram_bf=True)
                def _t(t):
                    mk.wait(t, wm[0], wm[1])
                    if k2 == 0:
                        mk.wait(t, hid_done[0], hid_done[1])
                        mk.wait(t, "v", mk.n["v"])
                        mk.wait(t, "s", mk.n["s"])
                    for si in range(2):
                        for ofg in range(4):
                            dst = (PBIG[:, ofg * 512:(ofg + 1) * 512]
                                   if si == 0 else sec1[ofg])
                            ins = t.matmul(
                                dst,
                                hid[:, k2 * 256 + si * 128:
                                    k2 * 256 + si * 128 + 128],
                                wslab[:, (k2 % 2) * D + ofg * 512:
                                      (k2 % 2) * D + ofg * 512 + 512],
                                start=(k2 == 0), stop=(k2 == 63))
                    mk.inc(ins, "t")
                mk.t(_t)
                slab_guard[k2 % 2] = mk.n["t"]
            tm3 = mk.n["t"]
            for si in range(2):
                s_ = gi * 2 + si
                def _v(v):
                    if si == 0:
                        mk.wait(v, "t", tm3)
                        v.tensor_add(stripe_a[:],
                                     big_b[:, s_ * D:(s_ + 1) * D],
                                     PBIG[:, 0:D])
                    else:
                        v.drain()
                        for ofg in range(4):
                            v.tensor_add(
                                stripe_a[:, ofg * 512:(ofg + 1) * 512],
                                big_b[:, s_ * D + ofg * 512:
                                      s_ * D + ofg * 512 + 512],
                                sec1[ofg])
                    v.drain()
                    mk.inc(v.tensor_reduce(st["sum"][:, 0:1], stripe_a[:],
                                           AX.X, ALU.add), "v")
                mk.v(_v)
                vbo = mk.n["v"]
                if stop_after == "DBG5" and gi == 0 and si == 0:
                    def _yd(y):
                        mk.wait(y, "v", vbo)
                        mk.inc(y.dma_start(out=out[0:128, :],
                                           in_=stripe_a[:]), "y", 16)
                        mk.wait(y, "y", mk.n["y"])
                    mk.y(_yd)
                    def _gd(g):
                        mk.wait(g, "v", vbo)
                        mk.inc(g.dma_start(
                            out=out[128:1024, 0:2048]
                            .rearrange("(a p) d -> p a d", p=128),
                            in_=big_a[:, 0:7 * D]
                            .rearrange("p (a d) -> p a d", d=D)), "gm", 16)
                        mk.wait_dma(g, "gm")
                    mk.g(_gd)
                    mk.rendezvous()
                    return nc
                def _sq(s):
                    mk.wait(s, "v", vbo)
                    mk.inc(s.activation(scr_f[:, 0:D], stripe_a[:],
                                        AF.Square), "s")
                mk.s(_sq)
                sqm = mk.n["s"]
                def _v1b(v):
                    mk.wait(v, "s", sqm)
                    v.tensor_reduce(st["sq"][:, 0:1], scr_f[:, 0:D], AX.X,
                                    ALU.add)
                    c_ = 1.0 / D
                    v.tensor_scalar(st["tmp"][:, 0:1], st["sum"][:, 0:1], c_,
                                    None, ALU.mult)
                    v.drain()
                    v.tensor_scalar(st["tmp2"][:, 0:1], st["sq"][:, 0:1], c_,
                                    None, ALU.mult)
                    v.tensor_mul(st["minv"][:, 0:1], st["tmp"][:, 0:1],
                                 st["tmp"][:, 0:1])
                    v.drain()
                    mk.inc(v.tensor_sub(st["tmp2"][:, 0:1], st["tmp2"][:, 0:1],
                                        st["minv"][:, 0:1]), "v")
                mk.v(_v1b)
                vm3 = mk.n["v"]
                def _sm(s):
                    mk.wait(s, "v", vm3)
                    s.activation(st["tmp2"][:, 0:1], st["tmp2"][:, 0:1],
                                 AF.Ln, bias=cb(EPS_LN))
                    s.drain()
                    mk.inc(s.activation(st["tmp2"][:, 0:1], st["tmp2"][:, 0:1],
                                        AF.Exp, scale=-0.5), "s")
                mk.s(_sm)
                sm3 = mk.n["s"]
                def _vr(v):
                    mk.wait(v, "s", sm3)
                    v.tensor_mul(st["minv"][:, 0:1], st["tmp"][:, 0:1],
                                 st["tmp2"][:, 0:1])
                    v.drain()
                    v.tensor_scalar(x_tm[:, s_ * D:(s_ + 1) * D],
                                    x_tm[:, s_ * D:(s_ + 1) * D],
                                    st["minv"][:, 0:1], None, ALU.subtract)
                    v.drain()
                    mk.inc(v.scalar_tensor_tensor(
                        x_tm[:, s_ * D:(s_ + 1) * D], stripe_a[:],
                        st["tmp2"][:, 0:1], x_tm[:, s_ * D:(s_ + 1) * D],
                        ALU.mult, ALU.add), "v")
                mk.v(_vr)

        mk.rendezvous()

        if stop_after == "L0_attn":
            _finish(mk, out, x_tm)
            return nc

        vm, sm = lnstats(lambda s_: x_tm[:, s_ * D:(s_ + 1) * D])
        v_lnf = lnminv(st["m0"], st["i0"], wait_v=vm, wait_s=sm)
        for g2 in range(4):
            for si in range(2):
                s_ = g2 * 2 + si
                def _va(v):
                    if g2 == 0 and si == 0:
                        mk.wait(v, "v", v_lnf)
                    mk.wait(v, "s", pt_guard[0])
                    mk.inc(v.tensor_scalar(stripe_c[:],
                                           x_tm[:, s_ * D:(s_ + 1) * D],
                                           st["m0"][:, s_:s_ + 1],
                                           st["i0"][:, s_:s_ + 1],
                                           ALU.subtract, ALU.mult), "v")
                mk.v(_va)
                va = mk.n["v"]
                transpose_stripe(lambda c: stripe_c[:, c * 128:(c + 1) * 128],
                                 si, xT, wait_t=(("v", va),))
            xnT_done = mk.n["s"]
            pfg = [mk.n["s"], mk.n["s"]]
            for hc in range(128):
                wm = load_kslab(whgu, hc * 128, 128, hc % 2)
                def _t(t):
                    mk.wait(t, wm[0], wm[1])
                    if hc == 0:
                        mk.wait(t, "s", xnT_done)
                        mk.wait(t, "v", mk.n["v"])
                    mk.wait(t, "s", pfg[hc % 2])
                    b = hc % 2
                    for k in range(FC):
                        ins = t.matmul(
                            PF[:, b * 512:b * 512 + 256],
                            wslab[:, b * D + k * 128:b * D + k * 128 + 128],
                            xT[:, k * TL:k * TL + 256],
                            start=(k == 0), stop=(k == FC - 1))
                    mk.inc(ins, "t")
                mk.t(_t)
                tm = mk.n["t"]
                slab_guard[hc % 2] = tm
                def _s(s):
                    mk.wait(s, "t", tm)
                    mk.wait(s, "v", mk.n["v"])
                    b = hc % 2
                    if hc < 64:
                        ins = s.activation(big_b[:, hc * 256:(hc + 1) * 256],
                                           PF[:, b * 512:b * 512 + 256],
                                           AF.Silu)
                    else:
                        ins = s.copy(big_a[:, (hc - 64) * 256:(hc - 63) * 256],
                                     PF[:, b * 512:b * 512 + 256])
                    mk.inc(ins, "s")
                mk.s(_s)
                pfg[hc % 2] = mk.n["s"]
            gu_done = mk.n["s"]
            def _vh(v):
                mk.wait(v, "s", gu_done)
                for q in range(4):
                    ins = v.tensor_mul(big_a[:, q * 4096:(q + 1) * 4096],
                                       big_b[:, q * 4096:(q + 1) * 4096],
                                       big_a[:, q * 4096:(q + 1) * 4096])
                mk.inc(v.drain(), "v")
            mk.v(_vh)
            h_done = mk.n["v"]
            tsec = [mk.n["t"]]
            for sec in range(4):
                def _sx(s):
                    if sec == 0:
                        mk.wait(s, "v", h_done)
                    mk.wait(s, "t", tsec[0])
                    mk.inc(s.activation(gla_t[:, 0:4096],
                                        big_a[:, sec * 4096:(sec + 1) * 4096],
                                        AF.Square), "s")
                mk.s(_sx)
                sxm = mk.n["s"]
                def _tx(t):
                    mk.wait(t, "s", sxm)
                    for i in range(16):
                        ins = t.matmul(PX[0:1, 0:256], ones_bf[:],
                                       gla_t[:, i * 256:(i + 1) * 256],
                                       start=(sec == 0 and i == 0),
                                       stop=(sec == 3 and i == 15))
                    mk.inc(ins, "t")
                mk.t(_tx)
                tsec[0] = mk.n["t"]
            def _se(s):
                mk.wait(s, "t", tsec[0])
                mk.inc(s.copy(sqrow[:, 0:256], PX[0:1, 0:256]), "s")
            mk.s(_se)
            sqm = mk.n["s"]
            for si in range(2):
                def _ti(t):
                    mk.wait(t, "s", sqm)
                    mk.wait(t, "v", mk.n["v"])
                    mk.inc(t.matmul(PX[0:128, 256 + si:257 + si],
                                    sqrow[0:1, si * 128:(si + 1) * 128],
                                    ones1[0:1, 0:1],
                                    start=True, stop=True), "t")
                mk.t(_ti)
                tim = mk.n["t"]
                def _vi(v):
                    mk.wait(v, "t", tim)
                    mk.inc(v.tensor_scalar(st["tmp"][:, 4 + si:5 + si],
                                           PX[0:128, 256 + si:257 + si],
                                           1.0 / H1, EPS_RMS,
                                           ALU.mult, ALU.add), "v")
                mk.v(_vi)
                vim = mk.n["v"]
                def _si(s):
                    mk.wait(s, "v", vim)
                    s.activation(st["tmp"][:, 4 + si:5 + si],
                                 st["tmp"][:, 4 + si:5 + si], AF.Ln)
                    s.drain()
                    mk.inc(s.activation(st["tmp"][:, 4 + si:5 + si],
                                        st["tmp"][:, 4 + si:5 + si],
                                        AF.Exp, scale=-0.5), "s")
                mk.s(_si)
            sim = mk.n["s"]
            def _vi2(v):
                mk.wait(v, "s", sim)
                mk.inc(v.drain(), "v")
            mk.v(_vi2)
            v_inv = mk.n["v"]
            for k2 in range(64):
                wm = load_rowslab(whp, k2 * 128, D, k2 % 2, dram_bf=True)
                def _t(t):
                    mk.wait(t, wm[0], wm[1])
                    if k2 == 0:
                        mk.wait(t, "v", v_inv)
                        mk.wait(t, "s", mk.n["s"])
                    for si in range(2):
                        for ofg in range(4):
                            dst = (PBIG[:, ofg * 512:(ofg + 1) * 512]
                                   if si == 0 else sec1[ofg])
                            ins = t.matmul(
                                dst,
                                big_a[:, k2 * 256 + si * 128:
                                      k2 * 256 + si * 128 + 128],
                                wslab[:, (k2 % 2) * D + ofg * 512:
                                      (k2 % 2) * D + ofg * 512 + 512],
                                start=(k2 == 0), stop=(k2 == 63))
                    mk.inc(ins, "t")
                mk.t(_t)
                slab_guard[k2 % 2] = mk.n["t"]
            tm3 = mk.n["t"]
            for si in range(2):
                s_ = g2 * 2 + si
                def _ve(v):
                    if si == 0:
                        mk.wait(v, "t", tm3)
                        mk.inc(v.tensor_scalar(stripe_a[:], PBIG[:, 0:D],
                                               st["tmp"][:, 4:5], None,
                                               ALU.mult), "v")
                    else:
                        v.drain()
                        for ofg in range(4):
                            ins = v.tensor_scalar(
                                stripe_a[:, ofg * 512:(ofg + 1) * 512],
                                sec1[ofg], st["tmp"][:, 5:6], None,
                                ALU.mult)
                        mk.inc(ins, "v")
                mk.v(_ve)
                ln_post_stripe(s_, mk.n["v"])

        mk.rendezvous()
        if stop_after == "L0_ffn":
            _finish(mk, out, x_tm)
            return nc

        KT0, KT1, AOF, SGOF, OGT = 8192, 9216, 10240, 10368, 12416
        vm, sm = lnstats(lambda s_: x_tm[:, s_ * D:(s_ + 1) * D])
        v_ln1a = lnminv(st["m0"], st["i0"], wait_v=vm, wait_s=sm)
        for s_ in range(NS):
            def _va(v):
                if s_ == 0:
                    mk.wait(v, "v", v_ln1a)
                mk.wait(v, "s", pt_guard[0])
                mk.inc(v.tensor_scalar(stripe_c[:],
                                       x_tm[:, s_ * D:(s_ + 1) * D],
                                       st["m0"][:, s_:s_ + 1],
                                       st["i0"][:, s_:s_ + 1],
                                       ALU.subtract, ALU.mult), "v")
            mk.v(_va)
            va = mk.n["v"]
            transpose_stripe(lambda c: stripe_c[:, c * 128:(c + 1) * 128], s_,
                             xT, wait_t=(("v", va),))
        xnT_done = mk.n["s"]

        def _gw(g):
            mk.inc(g.dma_start(out=wgk2_sb[:],
                               in_=wgk2.rearrange("a (r n) -> a r n", n=128)),
                   "gm", 16)
        mk.g(_gw)
        wmz = load_kslab(wgk1, 0, 16, 0)
        for half in range(2):
            def _tz(t, half=half):
                if half == 0:
                    mk.wait(t, wmz[0], wmz[1])
                    mk.wait(t, "s", xnT_done)
                else:
                    mk.wait(t, "s", mk.n["s"])
                for k in range(FC):
                    ins = t.matmul(
                        PX[0:16, 0:512],
                        wslab[:, k * 16:k * 16 + 16],
                        xT[:, k * TL + half * 512:k * TL + half * 512 + 512],
                        start=(k == 0), stop=(k == FC - 1))
                mk.inc(ins, "t")
            mk.t(_tz)
            tzm = mk.n["t"]
            if half == 1:
                slab_guard[0] = tzm
            def _sz(s, half=half, tzm=tzm):
                mk.wait(s, "t", tzm)
                mk.inc(s.copy(z1_sb[0:16, half * 512:half * 512 + 512],
                              PX[0:16, 0:512]), "s")
            mk.s(_sz)
        z1_done = mk.n["s"]

        for which, w_ap, base in (("q", wq1, 0), ("k", wk1, 8192)):
            for qc in range(8):
                wm = load_kslab(w_ap, qc * 128, 128, qc % 2)
                def _t(t):
                    mk.wait(t, wm[0], wm[1])
                    mk.wait(t, "s", mk.n["s"])
                    for half in range(2):
                        for k in range(FC):
                            ins = t.matmul(
                                PF[:, half * 512:(half + 1) * 512],
                                wslab[:, (qc % 2) * D + k * 128:
                                      (qc % 2) * D + k * 128 + 128],
                                xT[:, k * TL + half * 512:
                                   k * TL + half * 512 + 512],
                                start=(k == 0), stop=(k == FC - 1))
                    mk.inc(ins, "t")
                mk.t(_t)
                tqm = mk.n["t"]
                slab_guard[qc % 2] = tqm
                def _s(s):
                    mk.wait(s, "t", tqm)
                    ins = s.copy(
                        big_a[:, base:base + 8192]
                        .rearrange("p (c n) -> p c n", n=1024)
                        [:, :, qc * 128:(qc + 1) * 128],
                        PF[:, 0:1024].rearrange("p (c n) -> p c n", n=128))
                    mk.inc(ins, "s")
                mk.s(_s)
        qk_done = mk.n["s"]

        def _vz(v):
            mk.wait(v, "s", qk_done)
            v.memset(st["Sp"][:], 0.0)
            v.memset(S_run[:], 0.0)
            mk.inc(v.drain(), "v")
        mk.v(_vz)

        def gla_gk_G(c, extra_wait_v=0):
            def _vrp(v):
                mk.wait(v, "v", extra_wait_v)
                v.drain()
                v.tensor_copy(st["Mp"][:], st["Sp"][:])
                mk.inc(v.drain(), "v")
            mk.v(_vrp)
            vrp = mk.n["v"]
            def _tz(t):
                mk.wait(t, "s", z1_done)
                mk.wait_dma(t, "gm")
                mk.wait(t, "v", mk.n["v"])
                mk.wait(t, "s", mk.n["s"])
                for r in range(8):
                    ins = t.matmul(PF[:, r * 128:(r + 1) * 128],
                                   wgk2_sb[0:16, r * 128:(r + 1) * 128],
                                   z1_sb[0:16, c * 128:(c + 1) * 128],
                                   start=True, stop=True)
                mk.inc(ins, "t")
            mk.t(_tz)
            tzm = mk.n["t"]
            def _sa(s):
                mk.wait(s, "t", tzm)
                mk.wait(s, "v", mk.n["v"])
                for r in range(8):
                    ins = s.activation(
                        scr_f[:, 1024 + r * 128:1024 + (r + 1) * 128],
                        PF[:, r * 128:(r + 1) * 128], AF.Abs,
                        bias=vecs[:, 6 * FC + r:6 * FC + r + 1])
                s.drain()
                for r in range(8):
                    ins = s.activation(
                        scr_f[:, 1024 + r * 128:1024 + (r + 1) * 128],
                        scr_f[:, 1024 + r * 128:1024 + (r + 1) * 128],
                        AF.Exp, scale=-1.0)
                s.drain()
                for r in range(8):
                    ins = s.activation(
                        scr_f[:, 1024 + r * 128:1024 + (r + 1) * 128],
                        scr_f[:, 1024 + r * 128:1024 + (r + 1) * 128],
                        AF.Ln, bias=cb(1.0))
                mk.inc(ins, "s")
            mk.s(_sa)
            sam = mk.n["s"]
            def _vg(v):
                mk.wait(v, "s", sam)
                mk.wait(v, "v", vrp)
                for r in range(8):
                    v.tensor_scalar(scr_f[:, r * 128:(r + 1) * 128],
                                    PF[:, r * 128:(r + 1) * 128],
                                    vecs[:, 6 * FC + r:6 * FC + r + 1], 0.0,
                                    ALU.add, ALU.min)
                v.drain()
                for r in range(8):
                    v.tensor_sub(scr_f[:, 1024 + r * 128:1024 + (r + 1) * 128],
                                 scr_f[:, r * 128:(r + 1) * 128],
                                 scr_f[:, 1024 + r * 128:1024 + (r + 1) * 128])
                v.drain()
                for r in range(8):
                    v.tensor_tensor_scan(
                        scr_f[:, r * 128:(r + 1) * 128],
                        ones1[:, 0:1].to_broadcast([128, 128]),
                        scr_f[:, 1024 + r * 128:1024 + (r + 1) * 128],
                        st["Mp"][:, r:r + 1], ALU.mult, ALU.add)
                v.drain()
                mk.inc(v.tensor_copy(
                    st["Sp"][:, 0:8],
                    scr_f[:, 0:1024].rearrange("p (r n) -> p r n", n=128)
                    [:, :, 127]), "v")
            mk.v(_vg)
            return mk.n["v"]

        for c in range(NS):
            vg = gla_gk_G(c)
            def _vcs(v):
                v.drain()
                mk.inc(v.tensor_copy(big_b[:, c * 1024:(c + 1) * 1024],
                                     S_run[:]), "v")
            mk.v(_vcs)
            def _vsc(v):
                v.tensor_sub(st["m1"][:], st["Sp"][:], st["Mp"][:])
                v.drain()
                mk.inc(v.tensor_scalar(st["d10b"][:], st["Sp"][:], 1.0 / 16,
                                       None, ALU.mult), "v")
            mk.v(_vsc)
            vsc = mk.n["v"]
            def _ssc(s):
                mk.wait(s, "v", vsc)
                s.activation(st["m1"][:], st["m1"][:], AF.Exp, scale=1.0 / 16)
                s.drain()
                for r in range(8):
                    ins = s.activation(
                        scr_f[:, 1024 + r * 128:1024 + (r + 1) * 128],
                        scr_f[:, r * 128:(r + 1) * 128], AF.Exp,
                        scale=-1.0 / 16,
                        bias=st["d10b"][:, r:r + 1])
                mk.inc(ins, "s")
            mk.s(_ssc)
            ssc = mk.n["s"]
            def _vk(v):
                mk.wait(v, "s", ssc)
                mk.wait(v, "s", mk.n["s"])
                mk.inc(v.tensor_mul(big_b[:, KT0:KT0 + 1024],
                                    big_a[:, 8192 + c * 1024:
                                          8192 + (c + 1) * 1024],
                                    scr_f[:, 1024:2048]), "v")
            mk.v(_vk)
            vk = mk.n["v"]
            for r0 in (0, 8):
                def _tt(t):
                    mk.wait(t, "v", vk)
                    mk.wait(t, "s", pt_guard[0])
                    for j in range(min(8, 8 - r0) if False else 8):
                        r = r0 + j
                        if r >= 8:
                            break
                        ins = t.transpose(PT[:, j * 128:(j + 1) * 128],
                                          big_b[:, KT0 + r * 128:
                                                KT0 + (r + 1) * 128],
                                          id_sb[:])
                    mk.inc(ins, "t")
                mk.t(_tt)
                ttm = mk.n["t"]
                def _st(s):
                    mk.wait(s, "t", ttm)
                    for j in range(8):
                        r = r0 + j
                        if r >= 8:
                            break
                        ins = s.copy(big_b[:, KT1 + r * 128:KT1 + (r + 1) * 128],
                                     PT[:, j * 128:(j + 1) * 128])
                    mk.inc(ins, "s")
                mk.s(_st)
                pt_guard[0] = mk.n["s"]
                if r0 == 0:
                    break
            khatT_done = mk.n["s"]
            for k in range(FC):
                wm = load_rowslab(wv1_bf, k * 128, D, k % 2, dram_bf=True)
                def _t(t):
                    mk.wait(t, wm[0], wm[1])
                    if k == 0:
                        mk.wait(t, "s", mk.n["s"])
                    for ofg in range(4):
                        ins = t.matmul(
                            PBIG[:, ofg * 512:(ofg + 1) * 512],
                            xT[:, k * TL + c * 128:k * TL + c * 128 + 128],
                            wslab[:, (k % 2) * D + ofg * 512:
                                  (k % 2) * D + ofg * 512 + 512],
                            start=(k == 0), stop=(k == FC - 1))
                    mk.inc(ins, "t")
                mk.t(_t)
                slab_guard[k % 2] = mk.n["t"]
            tvm = mk.n["t"]
            def _sv(s):
                mk.wait(s, "t", tvm)
                mk.wait(s, "v", mk.n["v"])
                mk.inc(s.copy(stripe_c[:], PBIG[:, 0:D]), "s")
            mk.s(_sv)
            sv = mk.n["s"]
            for r in range(8):
                def _td(t):
                    mk.wait(t, "s", khatT_done)
                    mk.wait(t, "s", sv)
                    mk.wait(t, "v", mk.n["v"])
                    for hh in range(2):
                        off = hh * 64
                        h = 2 * r + hh
                        ins = t.matmul(
                            PX[off:off + 64, 0:128],
                            big_b[:, KT1 + r * 128 + off:
                                  KT1 + r * 128 + off + 64],
                            stripe_c[:, h * 128:(h + 1) * 128],
                            start=True, stop=True,
                            tile_position=(0, off))
                    mk.inc(ins, "t")
                mk.t(_td)
                tdm = mk.n["t"]
                def _vd(v):
                    mk.wait(v, "t", tdm)
                    v.drain()
                    mk.inc(v.scalar_tensor_tensor(
                        S_run[:, r * 128:(r + 1) * 128],
                        S_run[:, r * 128:(r + 1) * 128],
                        st["m1"][:, r:r + 1],
                        PX[0:128, 0:128],
                        ALU.mult, ALU.add), "v")
                mk.v(_vd)
        v_sw0 = mk.n["v"]

        def _yp(y):
            mk.wait(y, "v", v_sw0)
            mk.inc(y.dma_start(out=b_gla_i[:, 0:1024], in_=S_run[:]), "y", 16)
            mk.inc(y.dma_start(out=b_gla_i[:, 1024:1032], in_=st["Sp"][:]),
                   "y", 16)
        mk.y(_yp)
        def _gg(g):
            mk.wait_dma(g, "y")
            if SKIP_CC:
                for j in range(NCORES):
                    mk.inc(g.dma_start(out=b_gla_o[j * 128:(j + 1) * 128, :],
                                       in_=b_gla_i[:, :]), "gm", 16)
                g.wait_ge(mk.h["gm"], mk.n["gm"])
                mk.inc(g.drain(), "c")
                return
            mk.inc(g.collective_compute(
                "AllGather", ALU.bypass, replica_groups=[list(range(NCORES))],
                ins=[b_gla_i.opt()], outs=[b_gla_o.opt()]), "c")
        mk.g(_gg)
        c_gla = mk.n["c"]
        def _vp0(v):
            mk.wait(v, "c", c_gla)
            v.memset(S_run[:], 0.0)
            v.memset(scr_f[:, 1024:2048], 0.0)
            mk.inc(v.drain(), "v")
        mk.v(_vp0)
        for j in range(7):
            def _gj(g):
                mk.wait(g, "c", c_gla)
                mk.wait(g, "v", mk.n["v"])
                mk.inc(g.dma_start(out=scr_f[:, 0:1024],
                                   in_=b_gla_o[j * 128:(j + 1) * 128, 0:1024]),
                       "gm", 16)
                mk.inc(g.dma_start(out=st["tmp2"][:, 0:8],
                                   in_=b_gla_o[j * 128:(j + 1) * 128,
                                               1024:1032]), "gm", 16)
            mk.g(_gj)
            def _sj(s):
                mk.wait_dma(s, "gm")
                mk.wait(s, "v", mk.n["v"])
                mk.inc(s.activation(st["tmp"][:, 0:8], st["tmp2"][:, 0:8],
                                    AF.Exp, scale=1.0 / 16), "s")
            mk.s(_sj)
            sjm = mk.n["s"]
            def _vj(v):
                mk.wait(v, "s", sjm)
                v.drain()
                for r in range(8):
                    v.scalar_tensor_tensor(
                        S_run[:, r * 128:(r + 1) * 128],
                        S_run[:, r * 128:(r + 1) * 128],
                        st["tmp"][:, r:r + 1],
                        scr_f[:, r * 128:(r + 1) * 128],
                        ALU.mult, ALU.add)
                v.drain()
                mk.inc(v.scalar_tensor_tensor(
                    scr_f[:, 1024:2048], S_run[:],
                    cmask_sb[:, j:j + 1], scr_f[:, 1024:2048],
                    ALU.mult, ALU.add), "v")
            mk.v(_vj)
        def _vpf(v):
            v.drain()
            v.tensor_copy(S_run[:], scr_f[:, 1024:2048])
            v.memset(st["Sp"][:], 0.0)
            mk.inc(v.drain(), "v")
        mk.v(_vpf)
        v_carry = mk.n["v"]

        for c in range(NS):
            vg = gla_gk_G(c, extra_wait_v=v_carry if c == 0 else 0)
            def _sl(s):
                mk.wait(s, "v", vg)
                mk.inc(s.activation(st["i1"][:], st["Mp"][:], AF.Exp,
                                    scale=1.0 / 16), "s")
            mk.s(_sl)
            slm = mk.n["s"]
            def _vsg(v):
                mk.wait(v, "s", slm)
                mk.wait(v, "t", mk.n["t"])
                for r in range(8):
                    ins = v.scalar_tensor_tensor(
                        big_b[:, KT1 + r * 128:KT1 + (r + 1) * 128],
                        S_run[:, r * 128:(r + 1) * 128],
                        st["i1"][:, r:r + 1],
                        big_b[:, c * 1024 + r * 128:c * 1024 + (r + 1) * 128],
                        ALU.mult, ALU.add)
                mk.inc(ins, "v")
            mk.v(_vsg)
            def _vbq(v):
                mk.inc(v.tensor_scalar(st["d10a"][:], st["Mp"][:], -1.0 / 16,
                                       -LN8, ALU.mult, ALU.add), "v")
            mk.v(_vbq)
            vbq = mk.n["v"]
            def _sq_(s):
                mk.wait(s, "v", vbq)
                for r in range(8):
                    ins = s.activation(
                        scr_f[:, 1024 + r * 128:1024 + (r + 1) * 128],
                        scr_f[:, r * 128:(r + 1) * 128], AF.Exp,
                        scale=1.0 / 16, bias=st["d10a"][:, r:r + 1])
                mk.inc(ins, "s")
            mk.s(_sq_)
            sqf = mk.n["s"]
            def _vqt(v):
                mk.wait(v, "s", sqf)
                mk.wait(v, "t", mk.n["t"])
                mk.inc(v.tensor_mul(gla_t[:, 2048:3072],
                                    big_a[:, c * 1024:(c + 1) * 1024],
                                    scr_f[:, 1024:2048]), "v")
            mk.v(_vqt)
            vqt = mk.n["v"]
            def _vbk(v):
                mk.inc(v.tensor_scalar(st["d10b"][:], st["Mp"][:], 1.0 / 16,
                                       None, ALU.mult), "v")
            mk.v(_vbk)
            vbk = mk.n["v"]
            def _sk(s):
                mk.wait(s, "v", vqt)
                mk.wait(s, "v", vbk)
                for r in range(8):
                    ins = s.activation(
                        scr_f[:, 1024 + r * 128:1024 + (r + 1) * 128],
                        scr_f[:, r * 128:(r + 1) * 128], AF.Exp,
                        scale=-1.0 / 16, bias=st["d10b"][:, r:r + 1])
                mk.inc(ins, "s")
            mk.s(_sk)
            skf = mk.n["s"]
            def _vkt(v):
                mk.wait(v, "s", skf)
                mk.wait(v, "t", mk.n["t"])
                mk.inc(v.tensor_mul(big_b[:, KT0:KT0 + 1024],
                                    big_a[:, 8192 + c * 1024:
                                          8192 + (c + 1) * 1024],
                                    scr_f[:, 1024:2048]), "v")
            mk.v(_vkt)
            vkt = mk.n["v"]
            for k in range(FC):
                wm = load_rowslab(wv1_bf, k * 128, D, k % 2, dram_bf=True)
                def _t(t):
                    mk.wait(t, wm[0], wm[1])
                    if k == 0:
                        mk.wait(t, "s", mk.n["s"])
                    for ofg in range(4):
                        ins = t.matmul(
                            PBIG[:, ofg * 512:(ofg + 1) * 512],
                            xT[:, k * TL + c * 128:k * TL + c * 128 + 128],
                            wslab[:, (k % 2) * D + ofg * 512:
                                  (k % 2) * D + ofg * 512 + 512],
                            start=(k == 0), stop=(k == FC - 1))
                    mk.inc(ins, "t")
                mk.t(_t)
                slab_guard[k % 2] = mk.n["t"]
            tvm = mk.n["t"]
            def _sv(s):
                mk.wait(s, "t", tvm)
                mk.wait(s, "v", mk.n["v"])
                mk.inc(s.copy(stripe_c[:], PBIG[:, 0:D]), "s")
            mk.s(_sv)
            sv = mk.n["s"]
            vsg_m = mk.n["v"]
            px_free = [mk.n["v"]]
            for h in range(16):
                r, off = h // 2, (h % 2) * 64
                def _ta(t):
                    mk.wait(t, "v", vkt)
                    mk.wait(t, "v", vqt)
                    mk.wait(t, "v", px_free[0])
                    mk.inc(t.matmul(
                        PX[0:128, 0:128],
                        big_b[off:off + 64, KT0 + r * 128:KT0 + (r + 1) * 128],
                        gla_t[off:off + 64, 2048 + r * 128:
                              2048 + (r + 1) * 128],
                        start=True, stop=True, tile_position=(off, 0)), "t")
                mk.t(_ta)
                tam = mk.n["t"]
                def _vam(v):
                    mk.wait(v, "t", tam)
                    mk.wait(v, "t", mk.n["t"])
                    mk.inc(v.tensor_mul(big_b[:, AOF:AOF + 128],
                                        PX[0:128, 0:128], masktri_sb[:]),
                           "v")
                mk.v(_vam)
                px_free[0] = mk.n["v"]
                vam = mk.n["v"]
                def _to(t):
                    mk.wait(t, "v", vam)
                    mk.wait(t, "s", sv)
                    mk.wait(t, "v", vsg_m)
                    t.matmul(PBIG[:, h * 128:(h + 1) * 128],
                             big_b[:, AOF:AOF + 128],
                             stripe_c[:, h * 128:(h + 1) * 128],
                             start=True, stop=False)
                    mk.inc(t.matmul(
                        PBIG[:, h * 128:(h + 1) * 128],
                        gla_t[off:off + 64, 2048 + r * 128:
                              2048 + (r + 1) * 128],
                        big_b[off:off + 64, KT1 + r * 128:KT1 + (r + 1) * 128],
                        start=False, stop=True, tile_position=(off, 0)), "t")
                mk.t(_to)
            to_done = mk.n["t"]
            def _vo(v):
                mk.wait(v, "t", to_done)
                v.tensor_copy(stripe_a[:], PBIG[:, 0:D])
                mk.inc(v.drain(), "v")
            mk.v(_vo)
            voc = mk.n["v"]
            def _so2(s):
                mk.wait(s, "v", voc)
                mk.inc(s.activation(scr_f[:, 0:D], stripe_a[:], AF.Square),
                       "s")
            mk.s(_so2)
            so2 = mk.n["s"]
            def _vo1(v):
                mk.wait(v, "s", so2)
                v.tensor_reduce(eloc[:], scr_f[:, 0:D]
                                .rearrange("p (h n) -> p h n", n=128),
                                AX.X, ALU.add)
                v.drain()
                mk.inc(v.tensor_scalar(ebuf[:], eloc[:], 1.0 / 128, EPS_LN,
                                       ALU.mult, ALU.add), "v")
            mk.v(_vo1)
            vom = mk.n["v"]
            if stop_after == "DBGGLA" and c == 0:
                def _yd(y):
                    mk.wait(y, "v", vom)
                    mk.inc(y.dma_start(out=out[0:128, :], in_=stripe_a[:]),
                           "y", 16)
                    mk.inc(y.dma_start(out=out[128:256, 0:16], in_=eloc[:]),
                           "y", 16)
                    mk.inc(y.dma_start(out=out[128:256, 16:32], in_=ebuf[:]),
                           "y", 16)
                    mk.inc(y.dma_start(out=out[256:384, 0:1024],
                                       in_=scr_f[:, 0:1024]), "y", 16)
                    mk.wait_dma(y, "y")
                mk.y(_yd)
                def _gd(g):
                    mk.wait(g, "v", vom)
                    mk.inc(g.dma_start(out=out[384:512, 0:1024],
                                       in_=gla_t[:, 2048:3072]), "gm", 16)
                    mk.inc(g.dma_start(out=out[512:640, 0:1024],
                                       in_=big_b[:, KT0:KT0 + 1024]), "gm", 16)
                    mk.inc(g.dma_start(out=out[640:768, 0:1024],
                                       in_=big_b[:, KT1:KT1 + 1024]), "gm", 16)
                    mk.inc(g.dma_start(out=out[768:896, 0:2048],
                                       in_=stripe_c[:]), "gm", 16)
                    mk.inc(g.dma_start(out=out[896:1024, 0:1024],
                                       in_=big_b[:, 0:1024]), "gm", 16)
                    mk.wait_dma(g, "gm")
                mk.g(_gd)
                mk.rendezvous()
                return nc
            def _so(s):
                mk.wait(s, "v", vom)
                s.activation(ebuf[:], ebuf[:], AF.Ln)
                s.drain()
                mk.inc(s.activation(ebuf[:], ebuf[:], AF.Exp, scale=-0.5),
                       "s")
            mk.s(_so)
            som = mk.n["s"]
            def _vo2(v):
                mk.wait(v, "s", som)
                v.tensor_mul(stripe_a[:]
                             .rearrange("p (h n) -> p h n", n=128),
                             stripe_a[:]
                             .rearrange("p (h n) -> p h n", n=128),
                             ebuf[:].unsqueeze(2).to_broadcast([128, 16, 128]))
                v.drain()
                mk.inc(v.tensor_mul(stripe_a[:], stripe_a[:], normw_b[:]),
                       "v")
            mk.v(_vo2)
            von = mk.n["v"]
            for k in range(FC):
                wm = load_rowslab(wg1_bf, k * 128, D, k % 2, dram_bf=True)
                def _t(t):
                    mk.wait(t, wm[0], wm[1])
                    if k == 0:
                        mk.wait(t, "v", von)
                    for ofg in range(4):
                        ins = t.matmul(
                            PBIG[:, ofg * 512:(ofg + 1) * 512],
                            xT[:, k * TL + c * 128:k * TL + c * 128 + 128],
                            wslab[:, (k % 2) * D + ofg * 512:
                                  (k % 2) * D + ofg * 512 + 512],
                            start=(k == 0), stop=(k == FC - 1))
                    mk.inc(ins, "t")
                mk.t(_t)
                slab_guard[k % 2] = mk.n["t"]
            tgm = mk.n["t"]
            def _sg(s):
                mk.wait(s, "t", tgm)
                mk.wait(s, "t", mk.n["t"])
                mk.inc(s.activation(big_b[:, SGOF:SGOF + D], PBIG[:, 0:D],
                                    AF.Silu), "s")
            mk.s(_sg)
            sgm = mk.n["s"]
            def _vog(v):
                mk.wait(v, "s", sgm)
                mk.inc(v.tensor_mul(big_b[:, SGOF:SGOF + D], stripe_a[:],
                                    big_b[:, SGOF:SGOF + D]), "v")
            mk.v(_vog)
            vog = mk.n["v"]
            for c0 in (0, 8):
                def _tt(t):
                    mk.wait(t, "v", vog)
                    mk.wait(t, "s", pt_guard[0])
                    for j in range(8):
                        cc2 = c0 + j
                        ins = t.transpose(PT[:, j * 128:(j + 1) * 128],
                                          big_b[:, SGOF + cc2 * 128:
                                                SGOF + (cc2 + 1) * 128],
                                          id_sb[:])
                    mk.inc(ins, "t")
                mk.t(_tt)
                ttm = mk.n["t"]
                def _st(s):
                    mk.wait(s, "t", ttm)
                    for j in range(8):
                        cc2 = c0 + j
                        ins = s.copy(big_b[:, OGT + cc2 * 128:
                                           OGT + (cc2 + 1) * 128],
                                     PT[:, j * 128:(j + 1) * 128])
                    mk.inc(ins, "s")
                mk.s(_st)
                pt_guard[0] = mk.n["s"]
            ogt_done = mk.n["s"]
            for k in range(FC):
                wm = load_rowslab(wo1_bf, k * 128, D, k % 2, dram_bf=True)
                def _t(t):
                    mk.wait(t, wm[0], wm[1])
                    if k == 0:
                        mk.wait(t, "s", ogt_done)
                        mk.wait(t, "v", mk.n["v"])
                    for ofg in range(4):
                        ins = t.matmul(
                            PBIG[:, ofg * 512:(ofg + 1) * 512],
                            big_b[:, OGT + k * 128:OGT + (k + 1) * 128],
                            wslab[:, (k % 2) * D + ofg * 512:
                                  (k % 2) * D + ofg * 512 + 512],
                            start=(k == 0), stop=(k == FC - 1))
                    mk.inc(ins, "t")
                mk.t(_t)
                slab_guard[k % 2] = mk.n["t"]
            twm = mk.n["t"]
            def _vbo(v):
                mk.wait(v, "t", twm)
                mk.inc(v.tensor_copy(stripe_a[:], PBIG[:, 0:D]), "v")
            mk.v(_vbo)
            ln_post_stripe(c, mk.n["v"])

        mk.rendezvous()
        if stop_after == "L1_attn":
            _finish(mk, out, x_tm)
            return nc

        def _gb1(g):
            mk.inc(g.dma_start(
                out=bias1_b, in_=b1.unsqueeze(0).to_broadcast([128, D])),
                "gm", 16)
        mk.g(_gb1)
        vm, sm = lnstats(lambda s_: x_tm[:, s_ * D:(s_ + 1) * D])
        v_lng = lnminv(st["m0"], st["i0"], wait_v=vm, wait_s=sm)
        for g2 in range(4):
            for si in range(2):
                s_ = g2 * 2 + si
                def _va(v):
                    if g2 == 0 and si == 0:
                        mk.wait(v, "v", v_lng)
                    mk.wait(v, "s", pt_guard[0])
                    mk.inc(v.tensor_scalar(stripe_c[:],
                                           x_tm[:, s_ * D:(s_ + 1) * D],
                                           st["m0"][:, s_:s_ + 1],
                                           st["i0"][:, s_:s_ + 1],
                                           ALU.subtract, ALU.mult), "v")
                mk.v(_va)
                va = mk.n["v"]
                transpose_stripe(lambda c: stripe_c[:, c * 128:(c + 1) * 128],
                                 si, xT, wait_t=(("v", va),))
            xnT_done = mk.n["s"]
            pfg = [mk.n["s"], mk.n["s"]]
            for oc in range(64):
                wm = load_kslab(w0t, oc * 128, 128, oc % 2)
                def _t(t):
                    mk.wait(t, wm[0], wm[1])
                    if oc == 0:
                        mk.wait(t, "s", xnT_done)
                        mk.wait(t, "v", mk.n["v"])
                    mk.wait(t, "s", pfg[oc % 2])
                    b = oc % 2
                    for k in range(FC):
                        ins = t.matmul(
                            PF[:, b * 512:b * 512 + 256],
                            wslab[:, b * D + k * 128:b * D + k * 128 + 128],
                            xT[:, k * TL:k * TL + 256],
                            start=(k == 0), stop=(k == FC - 1))
                    mk.inc(ins, "t")
                mk.t(_t)
                tm = mk.n["t"]
                slab_guard[oc % 2] = tm
                def _s(s):
                    mk.wait(s, "t", tm)
                    mk.wait(s, "t", mk.n["t"])
                    b = oc % 2
                    mk.inc(s.activation(big_a[:, oc * 256:(oc + 1) * 256],
                                        PF[:, b * 512:b * 512 + 256],
                                        AF.Gelu,
                                        bias=b0_sb[:, oc:oc + 1]), "s")
                mk.s(_s)
                pfg[oc % 2] = mk.n["s"]
            d_done = mk.n["s"]
            for k2 in range(64):
                wm = load_rowslab(w1t_bf, k2 * 128, D, k2 % 2, dram_bf=True)
                def _t(t):
                    mk.wait(t, wm[0], wm[1])
                    if k2 == 0:
                        mk.wait(t, "s", d_done)
                        mk.wait(t, "v", mk.n["v"])
                    for si in range(2):
                        for ofg in range(4):
                            dst = (PBIG[:, ofg * 512:(ofg + 1) * 512]
                                   if si == 0 else sec1[ofg])
                            ins = t.matmul(
                                dst,
                                big_a[:, k2 * 256 + si * 128:
                                      k2 * 256 + si * 128 + 128],
                                wslab[:, (k2 % 2) * D + ofg * 512:
                                      (k2 % 2) * D + ofg * 512 + 512],
                                start=(k2 == 0), stop=(k2 == 63))
                    mk.inc(ins, "t")
                mk.t(_t)
                slab_guard[k2 % 2] = mk.n["t"]
            tm3 = mk.n["t"]
            for si in range(2):
                s_ = g2 * 2 + si
                def _ve(v):
                    if si == 0:
                        mk.wait(v, "t", tm3)
                        mk.wait_dma(v, "gm")
                        mk.inc(v.tensor_add(stripe_a[:], PBIG[:, 0:D],
                                            bias1_b), "v")
                    else:
                        v.drain()
                        for ofg in range(4):
                            ins = v.tensor_add(
                                stripe_a[:, ofg * 512:(ofg + 1) * 512],
                                sec1[ofg],
                                bias1_b[:, ofg * 512:(ofg + 1) * 512])
                        mk.inc(ins, "v")
                mk.v(_ve)
                ln_post_stripe(s_, mk.n["v"])

        mk.rendezvous()
        _finish(mk, out, x_tm)
    return nc


def _finish(mk, out, x_tm):
    NS_, D_ = NS, D
    def _y(y):
        for s_ in range(NS_):
            mk.inc(y.dma_start(out=out[s_ * 128:(s_ + 1) * 128, :],
                               in_=x_tm[:, s_ * D_:(s_ + 1) * D_]), "y", 16)
        mk.wait_dma(y, "y")
    mk.y(_y)


_NC_CACHE = {}


def _consts():
    ident = np.eye(128, dtype=ml_dtypes.bfloat16)
    onesblk = np.zeros((128, 2), dtype=ml_dtypes.bfloat16)
    onesblk[0:64, 0] = 1
    onesblk[64:128, 1] = 1
    masktri = np.triu(np.ones((128, 128), dtype=np.float32)) \
        .astype(ml_dtypes.bfloat16)
    return ident, onesblk, masktri


def kernel(L0_attn_x, L0_attn_time_decay, L0_attn_time_first, L0_attn_Wk,
           L0_attn_Wv, L0_attn_Wr, L0_attn_Wo, L0_attn_Wfk, L0_attn_Wfv,
           L0_attn_ln1_w, L0_attn_ln1_b, L0_attn_ln2_w, L0_attn_ln2_b,
           L0_ffn_w_gate_up, L0_ffn_w_proj, L1_attn_Wq, L1_attn_Wk,
           L1_attn_Wv, L1_attn_Wg, L1_attn_Wgk1, L1_attn_Wgk2, L1_attn_bgk2,
           L1_attn_norm_w, L1_attn_Wo, L1_ffn_W0, L1_ffn_bias0, L1_ffn_W1,
           L1_ffn_bias1):
    key = STOP_AFTER
    if key not in _NC_CACHE:
        _NC_CACHE[key] = build_program(stop_after=STOP_AFTER)
    nc = _NC_CACHE[key]
    asf = lambda a: np.ascontiguousarray(np.asarray(a), dtype=np.float32)
    x = asf(L0_attn_x).reshape(8192, D)
    wgu_t = np.ascontiguousarray(asf(L0_ffn_w_gate_up).T)
    wp_t = np.ascontiguousarray(asf(L0_ffn_w_proj).T)
    w0_t = np.ascontiguousarray(asf(L1_ffn_W0).T)
    w1_t = np.ascontiguousarray(asf(L1_ffn_W1).T)
    ident, onesblk, masktri = _consts()
    base = dict(
        L0Wk=asf(L0_attn_Wk), L0Wv=asf(L0_attn_Wv), L0Wr=asf(L0_attn_Wr),
        L0Wo=asf(L0_attn_Wo), L0Wfk=asf(L0_attn_Wfk), L0Wfv=asf(L0_attn_Wfv),
        L0td=asf(L0_attn_time_decay).reshape(-1),
        L0tf=asf(L0_attn_time_first).reshape(-1),
        L0ln1w=asf(L0_attn_ln1_w), L0ln1b=asf(L0_attn_ln1_b),
        L0ln2w=asf(L0_attn_ln2_w), L0ln2b=asf(L0_attn_ln2_b),
        L1Wq=asf(L1_attn_Wq), L1Wk=asf(L1_attn_Wk), L1Wv=asf(L1_attn_Wv),
        L1Wg=asf(L1_attn_Wg), L1Wgk1=asf(L1_attn_Wgk1),
        L1Wgk2=asf(L1_attn_Wgk2), L1bgk2=asf(L1_attn_bgk2),
        L1normw=asf(L1_attn_norm_w), L1Wo=asf(L1_attn_Wo),
        W0T=w0_t, b0=asf(L1_ffn_bias0), W1T=w1_t, b1=asf(L1_ffn_bias1),
        ident=ident, onesblk=onesblk, masktri=masktri,
    )
    in_maps = []
    for i in range(NCORES):
        cm = np.zeros((128, 8), dtype=np.float32)
        if i >= 1:
            cm[:, i - 1] = 1.0
        m0 = np.full((128, 1), 1.0 if i == 0 else 0.0, dtype=np.float32)
        im = dict(base)
        im["x_sh"] = np.ascontiguousarray(x[i * TL:(i + 1) * TL])
        im["wgu_sl"] = np.ascontiguousarray(wgu_t[i * 256:(i + 1) * 256])
        im["wp_sl"] = np.ascontiguousarray(wp_t[i * 1024:(i + 1) * 1024])
        im["cmask"] = cm
        im["mask0"] = m0
        in_maps.append(im)
    res = run_bass_kernel_spmd(nc, in_maps, core_ids=list(range(NCORES)))
    full = np.concatenate([res.results[i]["out"] for i in range(NCORES)],
                          axis=0)
    return full.reshape(1, 8192, D)
